# version 17
# baseline (speedup 1.0000x reference)
import numpy as np
from contextlib import ExitStack

import concourse.bass as bass
import concourse.mybir as mybir
from concourse.bass_utils import run_bass_kernel_spmd
from concourse.hw_specs import get_activation_tables

F32 = mybir.dt.float32
BF16 = mybir.dt.bfloat16
FP8 = mybir.dt.float8e4
AF = mybir.ActivationFunctionType
ALU = mybir.AluOpType

D = 2048
L = 2048
INNER = 4096
FI = 6144
NDC = D // 128
NCC = INNER // 128
NIC = FI // 128
TH = 1024
NH = 2
EPS = 1e-5

ENGINES = ("g", "t", "s", "v", "y")
DBG_B_CONV = True
DBG_B_LOOP2 = True
DBG_B_ACTS = True
DBG_B_NB = 8
DBG_B_MM_ONLY = False
DBG_B_NOSLAB = False
DBG_C_LEVEL = 9
DBG_CONV_DEFER = 2

DMA_SEMS = ("gcn", "gx0", "gx1", "gx2", "gx3", "yr2", "yr3", "gi0", "gi1", "gg0", "gg1", "go0", "go1",
            "gdc0", "gdc1", "gq8", "gb0", "gb1", "gb2", "gb3", "gu2", "gu3", "gu8", "gu0", "gu1", "gd8", "gd0", "gd1",
            "ys0", "ys1", "yr0", "yr1", "yo0", "yo1")
CMP_SEMS = ("s_t", "s_s", "s_v", "gch")


class Sched:

    def __init__(self, nc, sems):
        self.nc = nc
        self.sems = sems
        self.cnt = {k: 0 for k in sems}
        self.ecnt = {"s_v": 0, "s_s": 0}
        self.q = {e: [] for e in ENGINES}

    ENG_SEM = {"v": "s_v", "s": "s_s"}

    def emit(self, eng, fn=None, waits=None, inc=None, inc_n=1, n=1):
        waits = {k: v for k, v in (waits or {}).items() if v > 0}
        for snm, v in waits.items():
            assert v <= self.cnt[snm], (
                f"wait on {snm}>={v} but only {self.cnt[snm]} scheduled")
        if eng in self.ENG_SEM:
            self.q[eng].append((waits, fn, None, 0))
            if fn is not None:
                self.cnt[self.ENG_SEM[eng]] += n
        else:
            self.q[eng].append((waits, fn, inc, inc_n))
            if inc is not None:
                self.cnt[inc] += inc_n
        return dict(self.cnt)

    def barrier(self, exclude=()):
        snap = dict(self.cnt)
        for e in ENGINES:
            if e not in exclude:
                self.emit(e, None, waits=snap)

    def guard(self):
        return {k: self.cnt[k] for k in ("s_t", "s_s", "s_v")}

    def flush(self):
        nc = self.nc
        qs = self.q
        self.q = {e: [] for e in ENGINES}
        sems = self.sems

        def mk(entries, chain_sem):
            def run(eng):
                if chain_sem is None:
                    for waits, fn, inc, inc_n in entries:
                        for snm, v in waits.items():
                            eng.wait_ge(sems[snm], v)
                        if fn is not None:
                            instr = fn(eng)
                            if inc is not None:
                                instr.then_inc(sems[inc], inc_n)
                    return
                proxy = _ChainProxy(eng, sems[chain_sem], self.ecnt, chain_sem)
                for waits, fn, inc, inc_n in entries:
                    for snm, v in waits.items():
                        eng.wait_ge(sems[snm], v)
                    if fn is not None:
                        fn(proxy)
            return run

        with nc.Block() as block:
            if qs["g"]:
                block.gpsimd(mk(qs["g"], None))
            if qs["t"]:
                block.tensor(mk(qs["t"], None))
            if qs["s"]:
                block.scalar(mk(qs["s"], "s_s"))
            if qs["v"]:
                block.vector(mk(qs["v"], "s_v"))
            if qs["y"]:
                block.sync(mk(qs["y"], None))
        for e, snm in self.ENG_SEM.items():
            assert self.ecnt[snm] == self.cnt[snm], (
                f"{snm}: emitted {self.ecnt[snm]} != scheduled {self.cnt[snm]}"
                " (check n= instruction counts)")


class _ChainProxy:
    def __init__(self, eng, sem, ecnt, key):
        self._eng = eng
        self._sem = sem
        self._ecnt = ecnt
        self._key = key

    def __getattr__(self, nm):
        f = getattr(self._eng, nm)
        if nm in ("wait_ge", "drain"):
            return f

        def g(*a, **k):
            n = self._ecnt[self._key]
            if n > 0:
                self._eng.wait_ge(self._sem, n)
            instr = f(*a, **k)
            instr.then_inc(self._sem, 1)
            self._ecnt[self._key] = n + 1
            return instr
        return g


def _load_act_set(eng, nc, set_name):
    tabs = list(get_activation_tables(nc.m.arch).keys())
    idx = tabs.index(set_name)
    return eng.add_instruction(mybir.InstLoadActFuncSet(
        name=nc.get_next_instruction_name(), act_func_set_id=idx, ins=[], outs=[]))


def build_nc(limit=99):
    nc = bass.Bass("TRN2", target_bir_lowering=False)

    x_h = nc.dram_tensor("x", [D, L], F32, kind="ExternalInput").ap()
    win_h = nc.dram_tensor("win", [NCC, D, 128], F32, kind="ExternalInput").ap()
    wgt_h = nc.dram_tensor("wgt", [NCC, D, 128], F32, kind="ExternalInput").ap()
    wout_h = nc.dram_tensor("wout", [NDC, INNER, 128], F32, kind="ExternalInput").ap()
    wg_h = nc.dram_tensor("wg", [NIC, D, 128], F32, kind="ExternalInput").ap()
    wu_h = nc.dram_tensor("wu", [NIC, D, 128], F32, kind="ExternalInput").ap()
    wd_h = nc.dram_tensor("wd", [NDC, FI, 128], F32, kind="ExternalInput").ap()
    dtw_h = nc.dram_tensor("dtw", [NCC, 128, 128], F32, kind="ExternalInput").ap()
    dcv_h = nc.dram_tensor("dcv", [NCC, 4, 128, 128], F32, kind="ExternalInput").ap()
    chanv_h = nc.dram_tensor("chanv", [128, 320], F32, kind="ExternalInput").ap()
    gsc_h = nc.dram_tensor("gsc", [128, NIC], F32, kind="ExternalInput").ap()
    usc_h = nc.dram_tensor("usc", [128, NIC], F32, kind="ExternalInput").ap()
    dsc_h = nc.dram_tensor("dsc", [128, NDC], F32, kind="ExternalInput").ap()
    out_h = nc.dram_tensor("out", [D, L], F32, kind="ExternalOutput").ap()
    x2scr_h = nc.dram_tensor("x2scr", [D, L], BF16, kind="Internal").ap()
    wg8_h = nc.dram_tensor("wg8s", [NIC, D, 128], FP8, kind="Internal").ap()
    wu8_h = nc.dram_tensor("wu8s", [NIC, D, 128], FP8, kind="Internal").ap()
    wd8_h = nc.dram_tensor("wd8s", [NDC, FI, 128], FP8, kind="Internal").ap()

    st = ExitStack()
    S_ = st.enter_context
    with st:
        sems = {}
        for nm in DMA_SEMS + CMP_SEMS:
            sems[nm] = S_(nc.semaphore(nm))
        sch = Sched(nc, sems)

        chanv = S_(nc.sbuf_tensor("chanv_s", [128, 320], F32, side="right"))
        gsc = S_(nc.sbuf_tensor("gsc_s", [128, NIC], F32, side="right"))
        usc = S_(nc.sbuf_tensor("usc_s", [128, NIC], F32, side="right"))
        dsc = S_(nc.sbuf_tensor("dsc_s", [128, NDC], F32, side="right"))
        dtwst = S_(nc.sbuf_tensor("dtwst", [128, NCC * 128], BF16, side="right"))
        negA = S_(nc.sbuf_tensor("negA", [128, NCC], F32, side="right"))
        Cp2 = S_(nc.sbuf_tensor("Cp2", [128, NCC], F32, side="right"))
        Dp2 = S_(nc.sbuf_tensor("Dp2", [128, NCC], F32, side="right"))
        ones_bf = S_(nc.sbuf_tensor("ones_bf", [128, 128], BF16, side="right"))
        carry = S_(nc.sbuf_tensor("carry", [128, NCC], F32, side="right"))
        tails = S_(nc.sbuf_tensor("tails", [128, NCC * 3], BF16, side="right"))
        etmp = S_(nc.sbuf_tensor("etmp", [128, NCC], F32, side="right"))
        cst = S_(nc.sbuf_tensor("cst", [128, 2], F32, side="right"))
        xrot = S_(nc.sbuf_tensor("xrot_p", [128, 4 * TH], BF16, side="right"))
        xsq = S_(nc.sbuf_tensor("xsq_p", [128, 2 * TH], BF16, side="right"))
        nc.const_aps.register(F32, 0.5, cst[:, 0:1])
        nc.const_aps.register(F32, -0.5, cst[:, 1:2])

        pp = [S_(nc.psum_tensor(f"pp{j}", [128, 1024], F32)) for j in range(4)]

        def cv(group, cc):
            return chanv[:, group * 32 + cc: group * 32 + cc + 1]

        sch.emit("g", lambda g: g.dma_start(chanv[:, :], chanv_h),
                 inc="gcn", inc_n=16)
        sch.emit("g", lambda g: g.dma_start(gsc[:, :], gsc_h), inc="gcn", inc_n=16)
        sch.emit("g", lambda g: g.dma_start(usc[:, :], usc_h), inc="gcn", inc_n=16)
        sch.emit("g", lambda g: g.dma_start(dsc[:, :], dsc_h), inc="gcn", inc_n=16)
        sch.emit("g", lambda g: g.dma_start(
            dtwst[:, :],
            bass.AP(dtw_h.tensor, 0, [[128, 128], [128 * 128, NCC], [1, 128]])),
            inc="gcn", inc_n=16)
        gd_const = sch.cnt["gcn"]
        sch.emit("v", lambda v: v.memset(ones_bf[:, :], 1.0), inc="s_v")
        sch.emit("v", lambda v: v.memset(cst[:, 0:1], 0.5), inc="s_v")
        sch.emit("v", lambda v: v.memset(cst[:, 1:2], -0.5), inc="s_v")
        sch.emit("v", lambda v: v.memset(carry[:, :], 0.0), inc="s_v")
        sch.emit("v", lambda v: v.memset(tails[:, :], 0.0), inc="s_v")
        sch.emit("s", lambda s: _load_act_set(s, nc, "natural_log_exp_and_others"),
                 waits={"gcn": gd_const})
        sch.emit("s", lambda s: s.activation(etmp[:, :], chanv[:, 0:32], AF.Exp),
                 inc="s_s")
        s_after_exp = sch.cnt["s_s"]
        sch.emit("v", lambda v: v.tensor_scalar_mul(negA[:, :], etmp[:, :], -1.0),
                 waits={"s_s": s_after_exp}, inc="s_v")
        sch.emit("v", lambda v: v.tensor_scalar_mul(Cp2[:, :], chanv[:, 64:96], 0.5),
                 inc="s_v")
        sch.emit("v", lambda v: v.tensor_scalar_mul(Dp2[:, :], chanv[:, 96:128], 0.5),
                 inc="s_v")
        sch.barrier()
        sch.flush()
        _flushed = [1]

        def _done():
            _flushed[0] += 1
            return _flushed[0] > limit

        def stats_math(bufs, p_sum, p_sq, t_rdy):
            m_b, va_b, vb_b, r_b, mr_b = bufs
            sch.emit("v", lambda v: v.tensor_scalar_mul(
                m_b[:, :], p_sum[:, :], 1.0 / D),
                waits={"s_t": t_rdy}, inc="s_v")
            sch.emit("v", lambda v: v.tensor_scalar_mul(
                va_b[:, :], p_sq[:, :], 1.0 / D), inc="s_v")
            sch.emit("v", lambda v: v.tensor_mul(
                vb_b[:, :], m_b[:, :], m_b[:, :]), inc="s_v")
            c = sch.emit("v", lambda v: v.scalar_tensor_tensor(
                va_b[:, :], va_b[:, :], EPS, vb_b[:, :],
                ALU.add, ALU.subtract), inc="s_v")
            sch.emit("s", lambda s: s.activation(
                vb_b[:, :], va_b[:, :], AF.Ln),
                waits={"s_v": c["s_v"]}, inc="s_s")
            c = sch.emit("s", lambda s: s.activation(
                r_b[:, :], vb_b[:, :], AF.Exp, scale=-0.5), inc="s_s")
            c = sch.emit("v", lambda v: v.tensor_mul(
                mr_b[:, :], m_b[:, :], r_b[:, :]),
                waits={"s_s": c["s_s"]}, inc="s_v")
            return c["s_v"]

        v_st3_prev = None
        for h in range(NH):
            t0 = h * TH

            comb_st = ExitStack()
            combined = comb_st.enter_context(
                nc.sbuf_tensor(f"comb{h}", [128, NCC * TH], BF16, side="right"))
            ab_st = ExitStack()
            ln1 = ab_st.enter_context(
                nc.sbuf_tensor(f"ln1_{h}", [128, NDC * TH], BF16))

            pgA = pg_xrot if h > 0 else sch.guard()
            with ExitStack() as a_st:
                AS = a_st.enter_context
                m_b = AS(nc.sbuf_tensor(f"m_b{h}", [128, TH], F32))
                va_b = AS(nc.sbuf_tensor(f"va_b{h}", [128, TH], F32))
                vb_b = AS(nc.sbuf_tensor(f"vb_b{h}", [128, TH], F32))
                r_b = AS(nc.sbuf_tensor(f"r_b{h}", [128, TH], BF16))
                mr_b = AS(nc.sbuf_tensor(f"mr_b{h}", [128, TH], BF16))
                xf32 = AS(nc.sbuf_tensor(f"xf32_{h}", [128, 4 * TH], F32))

                sch.emit("s", lambda s: _load_act_set(
                    s, nc, "natural_log_exp_and_others"))
                gcum = {}
                scnt = {}
                tcnt = {}
                for d in range(NDC):
                    sl = d % 4
                    w = dict(pgA) if d < 4 else {"s_t": tcnt[d - 4]}
                    c = sch.emit("g", (lambda d=d, sl=sl: lambda g: g.dma_start(
                        xrot[:, sl * TH:(sl + 1) * TH],
                        x_h[d * 128:(d + 1) * 128, t0:t0 + TH]))(),
                        waits=w, inc=f"gx{sl}", inc_n=16)
                    gcum[d] = c[f"gx{sl}"]
                    c = sch.emit("s", (lambda d=d, sl=sl: lambda s: s.activation(
                        xsq[:, (d % 2) * TH:(d % 2 + 1) * TH],
                        xrot[:, sl * TH:(sl + 1) * TH], AF.Square))(),
                        waits={f"gx{sl}": gcum[d],
                               **({"s_t": tcnt[d - 2]} if d >= 2 else {})},
                        inc="s_s")
                    scnt[d] = c["s_s"]

                    def stats_fn(d=d, sl=sl):
                        def fn(t):
                            last = None
                            for sub in range(2):
                                last = t.matmul(
                                    pp[2][:, sub * 512:(sub + 1) * 512],
                                    ones_bf[:, :],
                                    xrot[:, sl * TH + sub * 512:
                                         sl * TH + (sub + 1) * 512],
                                    start=(d == 0), stop=(d == NDC - 1))
                            for sub in range(2):
                                last = t.matmul(
                                    pp[3][:, sub * 512:(sub + 1) * 512],
                                    ones_bf[:, :],
                                    xsq[:, (d % 2) * TH + sub * 512:
                                        (d % 2) * TH + (sub + 1) * 512],
                                    start=(d == 0), stop=(d == NDC - 1))
                            return last
                        return fn
                    w = {f"gx{sl}": gcum[d], "s_s": scnt[d]}
                    if d == 0 and v_st3_prev is not None:
                        w["s_v"] = v_st3_prev
                    c = sch.emit("t", stats_fn(), waits=w, inc="s_t")
                    tcnt[d] = c["s_t"]

                v_stats = stats_math((m_b, va_b, vb_b, r_b, mr_b), pp[2], pp[3], tcnt[NDC - 1])

                vap = {}
                for d in range(NDC):
                    sl = d % 4
                    w = {} if d < 4 else {"s_v": vap[d - 4]}
                    c = sch.emit("y", (lambda d=d, sl=sl: lambda y: y.dma_start(
                        xf32[:, sl * TH:(sl + 1) * TH],
                        x_h[d * 128:(d + 1) * 128, t0:t0 + TH]))(),
                        waits=w, inc=f"yr{sl}", inc_n=16)
                    yx = c[f"yr{sl}"]

                    def apply_fn(d=d, sl=sl):
                        def fn(v):
                            v.tensor_mul(
                                xsq[:, (d % 2) * TH:(d % 2 + 1) * TH],
                                xf32[:, sl * TH:(sl + 1) * TH], r_b[:, :])
                            return v.tensor_sub(
                                ln1[:, d * TH:(d + 1) * TH],
                                xsq[:, (d % 2) * TH:(d % 2 + 1) * TH], mr_b[:, :])
                        return fn
                    c = sch.emit("v", apply_fn(),
                                 waits={f"yr{sl}": yx,
                                        **({"s_v": v_stats} if d < 2 else {})},
                                 n=2)
                    vap[d] = c["s_v"]
                sch.barrier(exclude=("g", "y", "t", "s"))
                sch.flush()
            if _done():
                ab_st.close(); comb_st.close(); break

            pgB = sch.guard()
            with ExitStack() as b_st:
                BS = b_st.enter_context
                inslab = BS(nc.sbuf_tensor(f"inslab{h}", [128, 2 * D], BF16))
                gslab = BS(nc.sbuf_tensor(f"gslab{h}", [128, 2 * D], BF16))
                cbuf = BS(nc.sbuf_tensor(f"cbuf{h}", [128, 3 * TH], F32))
                gth = BS(nc.sbuf_tensor(f"gth{h}", [128, 4 * TH], BF16))
                Xb = BS(nc.sbuf_tensor(f"Xb{h}", [128, 4 * TH], BF16))
                ebuf = BS(nc.sbuf_tensor(f"ebuf{h}", [128, 4 * TH], F32))
                dtbuf = BS(nc.sbuf_tensor(f"dtbuf{h}", [128, 2 * TH], F32))
                dkbuf = BS(nc.sbuf_tensor(f"dkbuf{h}", [128, 2 * TH], F32))
                drv = BS(nc.sbuf_tensor(f"drv{h}", [128, TH], F32))
                yAb = BS(nc.sbuf_tensor(f"yAb{h}", [128, 2 * TH], F32))

                t_in = {}; t_gate = {}; t_dt = {}; t_cv = {}
                conv_pend = []
                v_conv = {}; v_drv = {}; v_scan = {}; v_t = {}; v_comb = {}
                s_gth = {}; s_X = {}; s_e = {}; s_dk = {}; s_yA = {}
                g_in = {}; g_gt = {}

                def slab_dma(dst, slot, src3d, cc, nk):
                    def fn(g):
                        return g.dma_start(
                            dst[:, slot * nk * 128:(slot + 1) * nk * 128],
                            bass.AP(src3d.tensor,
                                    cc * nk * 128 * 128,
                                    [[128, 128], [128 * 128, nk], [1, 128]]))
                    return fn

                g_dc = {}

                def sched_slab(cc):
                    sl = cc % 2
                    w = dict(pgB) if cc < 2 else {"s_t": t_in[cc - 2]}
                    c = sch.emit("g", slab_dma(inslab, sl, win_h, cc, NDC),
                                 waits=w, inc=f"gi{sl}", inc_n=16)
                    g_in[cc] = c[f"gi{sl}"]
                    w = dict(pgB) if cc < 2 else {"s_t": t_gate[cc - 2]}
                    c = sch.emit("g", slab_dma(gslab, sl, wgt_h, cc, NDC),
                                 waits=w, inc=f"gg{sl}", inc_n=16)
                    g_gt[cc] = c[f"gg{sl}"]

                if not DBG_B_NOSLAB:
                    sched_slab(0)
                    sched_slab(1)

                def emit_dt(cc):
                    j = cc % 4

                    def dt_fn(cc=cc, j=j):
                        def fn(t):
                            last = None
                            for sub in range(2):
                                last = t.matmul(
                                    pp[cc % 2][:, sub * 512:(sub + 1) * 512],
                                    dtwst[:, cc * 128:(cc + 1) * 128],
                                    Xb[:, j * TH + sub * 512:
                                       j * TH + (sub + 1) * 512],
                                    start=True, stop=True)
                            return last
                        return fn
                    pair_rd = v_conv[cc + 2] if j < 2 else v_conv[cc]
                    w = {"s_s": s_X[cc], "s_v": pair_rd}
                    if cc >= 2:
                        w["s_s"] = max(w["s_s"], s_e[cc - 2])
                    c = sch.emit("t", dt_fn(), waits=w, inc="s_t")
                    t_dt[cc] = c["s_t"]

                def emit_silu(cc):
                    j = cc % 4
                    w = {"s_v": v_conv[cc]}
                    if cc >= 4:
                        w["s_t"] = t_dt[cc - 4]
                        w["s_v"] = max(w["s_v"], v_t[cc - 4])
                    c = sch.emit("s", (lambda cc=cc, j=j: lambda s:
                                  s.activation(Xb[:, j * TH:(j + 1) * TH],
                                               cbuf[:, (cc % 3) * TH:
                                                    (cc % 3 + 1) * TH],
                                               AF.Silu))(),
                                 waits=w, inc="s_s")
                    s_X[cc] = c["s_s"]

                def emit_e(cc):
                    j = cc % 4
                    w = {"s_t": t_dt[cc]}
                    c = sch.emit("s", (lambda cc=cc, j=j: lambda s: s.activation(
                        ebuf[:, j * TH:(j + 1) * TH], pp[cc % 2][:, :], AF.Exp,
                        bias=cv(4, cc)))(), waits=w, inc="s_s")
                    s_e[cc] = c["s_s"]

                def mm_group(pair, slab, slot, nk, mov):
                    def fn(t):
                        last = None
                        for k in range(nk):
                            for sub in range(2):
                                last = t.matmul(
                                    pp[pair][:, sub * 512:(sub + 1) * 512],
                                    slab[:, slot * nk * 128 + k * 128:
                                         slot * nk * 128 + (k + 1) * 128],
                                    mov[:, k * TH + sub * 512:
                                        k * TH + (sub + 1) * 512],
                                    start=(k == 0), stop=(k == nk - 1))
                        return last
                    return fn

                for b in range(DBG_B_NB):
                    sch.emit("s", lambda s: _load_act_set(s, nc, "silu_and_others"))
                    for j in range(4):
                        cc = b * 4 + j
                        if h == 0 and cc % 4 == 0:
                            ci = cc // 4
                            sch.emit("g", (lambda ci=ci: lambda g: g.dma_start(
                                bass.AP(wg8_h.tensor, ci * 6 * D * 128,
                                        [[1, 1], [1, 6 * D * 128]]),
                                bass.AP(wg_h.tensor, ci * 6 * D * 128,
                                        [[1, 1], [1, 6 * D * 128]])))(),
                                inc="gq8", inc_n=16)
                        if h == 0 and cc % 4 == 2:
                            ci = cc // 4
                            sch.emit("g", (lambda ci=ci: lambda g: g.dma_start(
                                bass.AP(wu8_h.tensor, ci * 6 * D * 128,
                                        [[1, 1], [1, 6 * D * 128]]),
                                bass.AP(wu_h.tensor, ci * 6 * D * 128,
                                        [[1, 1], [1, 6 * D * 128]])))(),
                                inc="gu8", inc_n=16)
                        w = {f"gi{cc % 2}": g_in[cc]}
                        if cc < 2:
                            w.update(pgB)
                        if cc >= 2:
                            w["s_v"] = v_conv.get(cc - 2, 0)
                            w["s_s"] = (s_e[cc - 2] if j < 2
                                        else s_X[cc - 2]) if not DBG_B_MM_ONLY else 0
                        c = sch.emit("t", mm_group(cc % 2, inslab, cc % 2,
                                                   NDC, ln1),
                                     waits=w, inc="s_t")
                        t_in[cc] = c["s_t"]
                        w = {f"gg{cc % 2}": g_gt[cc]}
                        if cc < 2:
                            w.update(pgB)
                        if cc >= 2:
                            w["s_s"] = s_gth.get(cc - 2, 0)
                        c = sch.emit("t", mm_group(2 + cc % 2, gslab, cc % 2,
                                                   NDC, ln1),
                                     waits=w, inc="s_t")
                        t_gate[cc] = c["s_t"]
                        if DBG_B_MM_ONLY:
                            if cc + 2 < NCC:
                                sched_slab(cc + 2)
                            continue
                        def conv_fn(cc=cc, j=cc % 3):
                            pj = pp[cc % 2]
                            cs = cbuf[:, j * TH:(j + 1) * TH]
                            def fn(v):
                                v.tensor_scalar(cs, pj[:, :],
                                                cv(9, cc), cv(5, cc),
                                                ALU.mult, ALU.add)
                                for k in range(3):
                                    sh = 3 - k
                                    v.scalar_tensor_tensor(
                                        cbuf[:, j * TH + sh:(j + 1) * TH],
                                        pj[:, 0:TH - sh], cv(6 + k, cc),
                                        cbuf[:, j * TH + sh:(j + 1) * TH],
                                        ALU.mult, ALU.add)
                                    v.scalar_tensor_tensor(
                                        cbuf[:, j * TH:j * TH + sh],
                                        tails[:, cc * 3 + k:cc * 3 + 3],
                                        cv(6 + k, cc),
                                        cbuf[:, j * TH:j * TH + sh],
                                        ALU.mult, ALU.add)
                                return v.tensor_copy(
                                    tails[:, cc * 3:cc * 3 + 3],
                                    pj[:, TH - 3:TH])
                            return fn
                        w = {"s_t": t_in[cc]}
                        if cc >= 3:
                            w["s_s"] = s_X[cc - 3]
                        c = sch.emit("v", conv_fn(), waits=w, n=8)
                        v_conv[cc] = c["s_v"]
                        t_cv[cc] = sch.cnt["s_t"]
                        emit_silu(cc)
                        if 1 <= cc < NCC - 1:
                            sched_slab(cc + 1)
                        w = {"s_t": t_gate[cc]}
                        if cc >= 4:
                            w["s_v"] = v_comb[cc - 4]
                        c = sch.emit("s", (lambda cc=cc, j=j: lambda s:
                                      s.activation(gth[:, j * TH:(j + 1) * TH],
                                                   pp[2 + cc % 2][:, :],
                                                   AF.Tanh, scale=0.5))(),
                                     waits=w, inc="s_s")
                        s_gth[cc] = c["s_s"]


                    if DBG_B_MM_ONLY:
                        continue
                    if not DBG_B_LOOP2:
                        for j in range(4):
                            cc = b * 4 + j
                            c = sch.emit("v", (lambda cc=cc, j=j: lambda v:
                                          v.tensor_copy(
                                              combined[:, cc * TH:(cc + 1) * TH],
                                              Xb[:, j * TH:(j + 1) * TH]))(),
                                         waits={"s_s": s_X[cc]}, inc="s_v")
                            v_comb[cc] = c["s_v"]
                            t_dt[cc] = sch.cnt["s_t"]
                            s_e[cc] = sch.cnt["s_s"]
                            v_t[cc] = c["s_v"]
                            v_scan[cc] = c["s_v"]
                            v_drv[cc] = c["s_v"]
                        continue
                    sch.emit("s", lambda s: _load_act_set(
                        s, nc, "natural_log_exp_and_others"))
                    for j in range(4):
                        cc = b * 4 + j
                        if j >= 2:
                            emit_e(cc - 2)
                        emit_dt(cc)
                    emit_e(b * 4 + 2)
                    emit_e(b * 4 + 3)
                    for j in range(4):
                        cc = b * 4 + j
                        w = {}
                        if cc >= 2:
                            w["s_v"] = v_drv[cc - 2]
                        c = sch.emit("s", (lambda cc=cc, j=j: lambda s:
                                      s.activation(
                                          dtbuf[:, (j % 2) * TH:(j % 2 + 1) * TH],
                                          ebuf[:, j * TH:(j + 1) * TH],
                                          AF.Ln, bias=1.0))(),
                            waits=w, inc="s_s")
                        w = {}
                        if cc >= 2:
                            w = {"s_v": v_scan[cc - 2]}
                        c = sch.emit("s", (lambda cc=cc, j=j: lambda s:
                                      s.activation(
                                          dkbuf[:, (cc % 2) * TH:(cc % 2 + 1) * TH],
                                          dtbuf[:, (j % 2) * TH:(j % 2 + 1) * TH],
                                          AF.Exp, scale=negA[:, cc:cc + 1]))(),
                            waits=w, inc="s_s")
                        s_dk[cc] = c["s_s"]
                        c = sch.emit("v", (lambda cc=cc, j=j: lambda v:
                                      v.scalar_tensor_tensor(
                                          drv[:, :],
                                          dtbuf[:, (j % 2) * TH:(j % 2 + 1) * TH],
                                          cv(1, cc),
                                          Xb[:, j * TH:(j + 1) * TH],
                                          ALU.mult, ALU.mult))(),
                                     waits={"s_s": s_dk[cc]}, inc="s_v")
                        v_drv[cc] = c["s_v"]
                        c = sch.emit("v", (lambda cc=cc: lambda v:
                                      v.tensor_tensor_scan(
                                          drv[:, :],
                                          dkbuf[:, (cc % 2) * TH:(cc % 2 + 1) * TH],
                                          drv[:, :],
                                          carry[:, cc:cc + 1],
                                          ALU.mult, ALU.add))(),
                                     inc="s_v")
                        if h == 0:
                            c = sch.emit("v", (lambda cc=cc: lambda v:
                                          v.tensor_copy(carry[:, cc:cc + 1],
                                                        drv[:, TH - 1:TH]))(),
                                         inc="s_v")
                        v_scan[cc] = c["s_v"]
                        w = {"s_v": v_scan[cc]}
                        if cc >= 2:
                            w["s_v"] = max(w["s_v"], v_t[cc - 2])
                        c = sch.emit("s", (lambda cc=cc: lambda s: s.activation(
                            yAb[:, (cc % 2) * TH:(cc % 2 + 1) * TH],
                            drv[:, :], AF.Identity,
                            scale=Cp2[:, cc:cc + 1]))(),
                            waits=w, inc="s_s")
                        s_yA[cc] = c["s_s"]
                        c = sch.emit("v", (lambda cc=cc, j=j: lambda v:
                                      v.scalar_tensor_tensor(
                                          yAb[:, (cc % 2) * TH:(cc % 2 + 1) * TH],
                                          Xb[:, j * TH:(j + 1) * TH],
                                          Dp2[:, cc:cc + 1],
                                          yAb[:, (cc % 2) * TH:(cc % 2 + 1) * TH],
                                          ALU.mult, ALU.add))(),
                                     waits={"s_s": s_yA[cc]}, inc="s_v")
                        v_t[cc] = c["s_v"]
                        c = sch.emit("v", (lambda cc=cc, j=j: lambda v:
                                      v.scalar_tensor_tensor(
                                          combined[:, cc * TH:(cc + 1) * TH],
                                          gth[:, j * TH:(j + 1) * TH], 1.0,
                                          yAb[:, (cc % 2) * TH:(cc % 2 + 1) * TH],
                                          ALU.add, ALU.mult))(),
                                     waits={"s_s": s_gth[cc]}, inc="s_v")
                        v_comb[cc] = c["s_v"]
                sch.barrier(exclude=("g", "y", "t", "s"))
                sch.flush()
            if _done():
                ab_st.close(); comb_st.close(); break

            pgC = sch.guard()
            ab_st.close()
            cd_st = ExitStack()
            ln2b = cd_st.enter_context(
                nc.sbuf_tensor(f"ln2b{h}", [128, NDC * TH], BF16))
            with ExitStack() as c_st:
                CS = c_st.enter_context
                attnb = CS(nc.sbuf_tensor(f"attnb{h}", [128, NDC * TH], BF16))
                xrot = xrot
                ost_st = ExitStack()
                ost = ost_st.enter_context(
                    nc.sbuf_tensor(f"ost{h}", [128, 2 * INNER], BF16,
                                   side="right"))
                sqrot = CS(nc.sbuf_tensor(f"sqrot{h}", [128, 2 * TH], BF16))
                sq2rot = CS(nc.sbuf_tensor(f"sq2rot{h}", [128, 2 * TH], BF16))
                m_b = CS(nc.sbuf_tensor(f"m_bC{h}", [128, TH], F32))
                va_b = CS(nc.sbuf_tensor(f"va_bC{h}", [128, TH], F32))
                vb_b = CS(nc.sbuf_tensor(f"vb_bC{h}", [128, TH], F32))
                r_b = CS(nc.sbuf_tensor(f"r_bC{h}", [128, TH], BF16))
                mr_b = CS(nc.sbuf_tensor(f"mr_bC{h}", [128, TH], BF16))

                sch.emit("s", lambda s: _load_act_set(
                    s, nc, "natural_log_exp_and_others"))
                g_o = {}; t_o = {}; v_ev = {}; s_sq = {}; t_st = {}
                stat_pend = []
                for d in range(NDC):
                    sl = d % 2
                    if h == 0 and d % 2 == 0:
                        ci = d // 2
                        sch.emit("g", (lambda ci=ci: lambda g: g.dma_start(
                            bass.AP(wd8_h.tensor, ci * 2 * FI * 128,
                                    [[1, 1], [1, 2 * FI * 128]]),
                            bass.AP(wd_h.tensor, ci * 2 * FI * 128,
                                    [[1, 1], [1, 2 * FI * 128]])))(),
                            inc="gd8", inc_n=16)
                    w = {} if d < 2 else {"s_t": t_o[d - 2]}
                    c = sch.emit("g", (lambda d=d, sl=sl: lambda g: g.dma_start(
                        ost[:, sl * INNER:(sl + 1) * INNER],
                        bass.AP(wout_h.tensor, d * INNER * 128,
                                [[128, 128], [128 * 128, NCC], [1, 128]])))(),
                        waits=w, inc=f"go{sl}", inc_n=16)
                    g_o[d] = c[f"go{sl}"]

                    def out_fn(d=d, sl=sl):
                        def fn(t):
                            last = None
                            for k in range(NCC):
                                for sub in range(2):
                                    last = t.matmul(
                                        pp[d % 2][:, sub * 512:(sub + 1) * 512],
                                        ost[:, sl * INNER + k * 128:
                                            sl * INNER + (k + 1) * 128],
                                        combined[:, k * TH + sub * 512:
                                                 k * TH + (sub + 1) * 512],
                                        start=(k == 0), stop=(k == NCC - 1))
                            return last
                        return fn
                    w = {f"go{sl}": g_o[d]}
                    if d < 2:
                        w.update(pgC)
                    if d >= 2:
                        w["s_v"] = v_ev[d - 2]
                        w["s_s"] = s_sq[d - 2]
                    c = sch.emit("t", out_fn(), waits=w, inc="s_t")
                    t_o[d] = c["s_t"]
                    c = sch.emit("v", (lambda d=d: lambda v: v.tensor_copy(
                        attnb[:, d * TH:(d + 1) * TH], pp[d % 2][:, :]))(),
                        waits={"s_t": t_o[d]}, inc="s_v")
                    v_ev[d] = c["s_v"]
                    if DBG_C_LEVEL < 2:
                        s_sq[d] = sch.cnt["s_s"]
                        t_st[d] = sch.cnt["s_t"]
                        continue
                    if DBG_C_LEVEL == 12:
                        w = {"s_t": t_o[d]}
                        c = sch.emit("s", (lambda d=d, sl=sl: lambda s:
                                      s.activation(
                                          sqrot[:, sl * TH:(sl + 1) * TH],
                                          pp[d % 2][:, :], AF.Square))(),
                                     waits=w, inc="s_s")
                        s_sq[d] = c["s_s"]
                        t_st[d] = sch.cnt["s_t"]
                        continue
                    if DBG_C_LEVEL == 13:
                        def statA_fn(d=d):
                            def fn(t):
                                last = None
                                for sub in range(2):
                                    last = t.matmul(
                                        pp[2][:, sub * 512:(sub + 1) * 512],
                                        ones_bf[:, :],
                                        attnb[:, d * TH + sub * 512:
                                              d * TH + (sub + 1) * 512],
                                        start=(d == 0), stop=(d == NDC - 1))
                                return last
                            return fn
                        c = sch.emit("t", statA_fn(),
                                     waits={"s_v": v_ev[d]}, inc="s_t")
                        t_st[d] = c["s_t"]
                        s_sq[d] = sch.cnt["s_s"]
                        continue
                    w = {"s_v": v_ev[d]}
                    if d >= 2:
                        w["s_t"] = t_st.get(d - 2) or t_st[d - 3]
                    c = sch.emit("s", (lambda d=d, sl=sl: lambda s: s.activation(
                        sqrot[:, sl * TH:(sl + 1) * TH],
                        attnb[:, d * TH:(d + 1) * TH],
                        AF.Square))(), waits=w, inc="s_s")
                    s_sq[d] = c["s_s"]

                    def stat_fn(d=d, sl=sl):
                        def fn(t):
                            last = None
                            for sub in range(2):
                                last = t.matmul(
                                    pp[2][:, sub * 512:(sub + 1) * 512],
                                    ones_bf[:, :],
                                    attnb[:, d * TH + sub * 512:
                                          d * TH + (sub + 1) * 512],
                                    start=(d == 0), stop=(d == NDC - 1))
                            for sub in range(2):
                                last = t.matmul(
                                    pp[3][:, sub * 512:(sub + 1) * 512],
                                    ones_bf[:, :],
                                    sqrot[:, sl * TH + sub * 512:
                                          sl * TH + (sub + 1) * 512],
                                    start=(d == 0), stop=(d == NDC - 1))
                            return last
                        return fn
                    stat_pend.append((d, stat_fn()))
                    if d >= 1:
                        pd, pfn = stat_pend.pop(0)
                        c = sch.emit("t", pfn,
                                     waits={"s_v": v_ev[pd], "s_s": s_sq[pd]},
                                     inc="s_t")
                        t_st[pd] = c["s_t"]

                if DBG_C_LEVEL >= 3:
                    pd, pfn = stat_pend.pop(0)
                c = sch.emit("t", pfn,
                             waits={"s_v": v_ev[pd], "s_s": s_sq[pd]},
                             inc="s_t")
                t_st[pd] = c["s_t"]
                stats_math((m_b, va_b, vb_b, r_b, mr_b), pp[2], pp[3], t_st[NDC - 1])

                ost_st.close()
                comb_st.close()
                x2b_st = ExitStack()
                x2b = x2b_st.enter_context(
                    nc.sbuf_tensor(f"x2b{h}", [128, NDC * TH], BF16,
                                   side="right"))

                xf32c = CS(nc.sbuf_tensor(f"xf32c{h}", [128, 2 * TH], F32))
                v_x2 = {}; s_sq2 = {}; t_st2 = {}; y_sp = {}
                for d in range(NDC if DBG_C_LEVEL >= 4 else 0):
                    sl = d % 2
                    w = dict(pgC) if d < 2 else {"s_v": v_x2[d - 2]}
                    c = sch.emit("y", (lambda d=d, sl=sl: lambda y: y.dma_start(
                        xf32c[:, sl * TH:(sl + 1) * TH],
                        x_h[d * 128:(d + 1) * 128, t0:t0 + TH]))(),
                        waits=w, inc=f"yr{sl}", inc_n=16)
                    gx = c[f"yr{sl}"]

                    def x2_fn(d=d, sl=sl):
                        def fn(v):
                            v.tensor_mul(sqrot[:, (d % 2) * TH:(d % 2 + 1) * TH],
                                         attnb[:, d * TH:(d + 1) * TH],
                                         r_b[:, :])
                            v.tensor_sub(sqrot[:, (d % 2) * TH:(d % 2 + 1) * TH],
                                         sqrot[:, (d % 2) * TH:(d % 2 + 1) * TH],
                                         mr_b[:, :])
                            return v.tensor_add(
                                x2b[:, d * TH:(d + 1) * TH],
                                sqrot[:, (d % 2) * TH:(d % 2 + 1) * TH],
                                xf32c[:, sl * TH:(sl + 1) * TH])
                        return fn
                    w = {f"yr{sl}": gx}
                    c = sch.emit("v", x2_fn(), waits=w, n=3)
                    v_x2[d] = c["s_v"]
                    w = {"s_v": v_x2[d]}
                    if d >= 2:
                        w["s_t"] = t_st2[d - 2]
                    c = sch.emit("s", (lambda d=d: lambda s: s.activation(
                        sq2rot[:, (d % 2) * TH:(d % 2 + 1) * TH],
                        x2b[:, d * TH:(d + 1) * TH], AF.Square))(),
                        waits=w, inc="s_s")
                    s_sq2[d] = c["s_s"]

                    def stat2_fn(d=d, sl=sl):
                        def fn(t):
                            last = None
                            for sub in range(2):
                                last = t.matmul(
                                    pp[0][:, sub * 512:(sub + 1) * 512],
                                    ones_bf[:, :],
                                    x2b[:, d * TH + sub * 512:
                                        d * TH + (sub + 1) * 512],
                                    start=(d == 0), stop=(d == NDC - 1))
                            for sub in range(2):
                                last = t.matmul(
                                    pp[1][:, sub * 512:(sub + 1) * 512],
                                    ones_bf[:, :],
                                    sq2rot[:, (d % 2) * TH + sub * 512:
                                           (d % 2) * TH + (sub + 1) * 512],
                                    start=(d == 0), stop=(d == NDC - 1))
                            return last
                        return fn
                    w = {"s_v": v_x2[d], "s_s": s_sq2[d]}
                    if d < 2:
                        w["s_v"] = max(w["s_v"], v_ev[NDC - 1])
                    c = sch.emit("t", stat2_fn(), waits=w, inc="s_t")
                    t_st2[d] = c["s_t"]
                    c = sch.emit("y", (lambda d=d: lambda y: y.dma_start(
                        x2scr_h[d * 128:(d + 1) * 128, t0:t0 + TH],
                        x2b[:, d * TH:(d + 1) * TH]))(),
                        waits={"s_v": v_x2[d]},
                        inc=f"ys{d % 2}", inc_n=16)
                    y_sp[d] = c[f"ys{d % 2}"]

                if DBG_C_LEVEL >= 5:
                    v_st2 = stats_math((m_b, va_b, vb_b, r_b, mr_b),
                                       pp[0], pp[1], t_st2[NDC - 1])

                v_l2 = {}
                for d in range(NDC if DBG_C_LEVEL >= 5 else 0):
                    sl = d % 2

                    def ln2_fn(d=d, sl=sl):
                        def fn(v):
                            v.tensor_mul(sq2rot[:, sl * TH:(sl + 1) * TH],
                                         x2b[:, d * TH:(d + 1) * TH],
                                         r_b[:, :])
                            return v.tensor_sub(
                                ln2b[:, d * TH:(d + 1) * TH],
                                sq2rot[:, sl * TH:(sl + 1) * TH],
                                mr_b[:, :])
                        return fn
                    c = sch.emit("v", ln2_fn(), waits={"s_v": v_st2}, n=2)
                    v_l2[d] = c["s_v"]
                pg_xrot = sch.guard()
                sch.barrier(exclude=("g", "y", "t", "s"))
                sch.flush()
            x2b_st.close()
            if _done():
                cd_st.close(); break

            pgD = sch.guard()
            fuse_st = ExitStack()
            fuse = fuse_st.enter_context(
                nc.sbuf_tensor(f"fuse{h}", [128, NIC * TH], BF16, side="right"))
            with ExitStack() as d_st:
                DS = d_st.enter_context
                g8 = DS(nc.sbuf_tensor(f"g8{h}", [128, 4 * D], FP8, side="right"))
                u8 = DS(nc.sbuf_tensor(f"u8{h}", [128, 4 * D], FP8, side="right"))
                gtmp = DS(nc.sbuf_tensor(f"gtmp{h}", [128, 2 * TH], BF16))
                utmp = DS(nc.sbuf_tensor(f"utmp{h}", [128, 2 * TH], BF16))

                sch.emit("s", lambda s: _load_act_set(s, nc, "silu_and_others"))
                gb_c = {}; ub_c = {}; t_g = {}; t_u = {}
                s_si = {}; v_ue = {}; v_fu = {}
                q_g = {}; q_u = {}

                def d_load(i):
                    sl = i % 4
                    w = {"gq8": 128}
                    if i >= 4:
                        w["s_t"] = t_g[i - 4]
                    c = sch.emit("y", (lambda i=i, sl=sl: lambda y: y.dma_start(
                        g8[:, sl * D:(sl + 1) * D],
                        bass.AP(wg8_h.tensor, i * D * 128,
                                [[128, 128], [128 * 128, NDC], [1, 128]])))(),
                        waits=w, inc=f"gb{sl}", inc_n=16)
                    gb_c[i] = c[f"gb{sl}"]
                    c = sch.emit("v", (lambda sl=sl: lambda v: v.tensor_copy(
                        g8[:, sl * D:(sl + 1) * D].bitcast(mybir.dt.uint32),
                        g8[:, sl * D:(sl + 1) * D].bitcast(mybir.dt.uint32)))(),
                        waits={f"gb{sl}": gb_c[i]})
                    q_g[i] = c["s_v"]
                    w = {"gu8": 128}
                    if i >= 4:
                        w["s_t"] = t_u[i - 4]
                    c = sch.emit("y", (lambda i=i, sl=sl: lambda y: y.dma_start(
                        u8[:, sl * D:(sl + 1) * D],
                        bass.AP(wu8_h.tensor, i * D * 128,
                                [[128, 128], [128 * 128, NDC], [1, 128]])))(),
                        waits=w, inc=f"gu{sl}", inc_n=16)
                    ub_c[i] = c[f"gu{sl}"]
                    c = sch.emit("v", (lambda sl=sl: lambda v: v.tensor_copy(
                        u8[:, sl * D:(sl + 1) * D].bitcast(mybir.dt.uint32),
                        u8[:, sl * D:(sl + 1) * D].bitcast(mybir.dt.uint32)))(),
                        waits={f"gu{sl}": ub_c[i]})
                    q_u[i] = c["s_v"]

                d_load(0)
                d_load(1)
                for i in range(NIC):
                    sl = i % 4
                    if i + 2 < NIC:
                        d_load(i + 2)

                    w = {f"gb{sl}": gb_c[i], "s_v": q_g[i]}
                    if i < 2:
                        w.update({k: max(w.get(k, 0), v) for k, v in pgD.items()})
                    if i >= 2:
                        w["s_s"] = s_si[i - 2]
                    c = sch.emit("t", mm_group(i % 2, g8, sl, NDC, ln2b),
                                 waits=w, inc="s_t")
                    t_g[i] = c["s_t"]
                    w = {f"gu{sl}": ub_c[i], "s_v": q_u[i]}
                    if i < 2:
                        w.update({k: max(w.get(k, 0), v) for k, v in pgD.items()})
                    if i >= 2:
                        w["s_v"] = max(w["s_v"], v_ue[i - 2])
                    c = sch.emit("t", mm_group(2 + i % 2, u8, sl, NDC, ln2b),
                                 waits=w, inc="s_t")
                    t_u[i] = c["s_t"]
                    w = {"s_t": t_g[i]}
                    if i >= 2:
                        w["s_v"] = v_fu[i - 2]
                    c = sch.emit("s", (lambda i=i: lambda s: s.activation(
                        gtmp[:, (i % 2) * TH:(i % 2 + 1) * TH], pp[i % 2][:, :],
                        AF.Silu, scale=gsc[:, i:i + 1]))(),
                        waits=w, inc="s_s")
                    s_si[i] = c["s_s"]
                    w = {"s_t": t_u[i]}
                    if i >= 2:
                        w["s_v"] = v_fu[i - 2]
                    c = sch.emit("v", (lambda i=i: lambda v:
                                  v.tensor_scalar_mul(
                                      utmp[:, (i % 2) * TH:(i % 2 + 1) * TH],
                                      pp[2 + i % 2][:, :], usc[:, i:i + 1]))(),
                                 waits=w, inc="s_v")
                    v_ue[i] = c["s_v"]
                    c = sch.emit("v", (lambda i=i: lambda v: v.tensor_mul(
                        fuse[:, i * TH:(i + 1) * TH],
                        gtmp[:, (i % 2) * TH:(i % 2 + 1) * TH],
                        utmp[:, (i % 2) * TH:(i % 2 + 1) * TH]))(),
                        waits={"s_s": s_si[i]}, inc="s_v")
                    v_fu[i] = c["s_v"]
                sch.barrier(exclude=("g", "y", "t", "s"))
                sch.flush()
            if _done():
                cd_st.close(); fuse_st.close(); break

            pgE = sch.guard()
            cd_st.close()
            with ExitStack() as e_st:
                ES = e_st.enter_context
                d8 = ES(nc.sbuf_tensor(f"d8{h}", [128, 2 * FI], FP8, side="right"))
                ffb = ES(nc.sbuf_tensor(f"ffb{h}", [128, NDC * TH], BF16))
                sqrot = ES(nc.sbuf_tensor(f"sqrotE{h}", [128, 2 * TH], BF16))
                x2rot = ES(nc.sbuf_tensor(f"x2rotE{h}", [128, 4 * TH], BF16))
                outst = ES(nc.sbuf_tensor(f"outst{h}", [128, 2 * TH], F32))
                m_b = ES(nc.sbuf_tensor(f"m_bE{h}", [128, TH], F32))
                va_b = ES(nc.sbuf_tensor(f"va_bE{h}", [128, TH], F32))
                vb_b = ES(nc.sbuf_tensor(f"vb_bE{h}", [128, TH], F32))
                r_b = ES(nc.sbuf_tensor(f"r_bE{h}", [128, TH], BF16))
                mr_b = ES(nc.sbuf_tensor(f"mr_bE{h}", [128, TH], BF16))

                sch.emit("s", lambda s: _load_act_set(s, nc, "silu_and_others"))
                gd_c = {}; t_d = {}; s_ff = {}; s_sq = {}; t_st = {}
                q_d = {}
                stat_pendE = []
                for d in range(NDC):
                    sl = d % 2
                    w = {"gd8": 128}
                    if d >= 2:
                        w["s_t"] = t_d[d - 2]
                    c = sch.emit("y", (lambda d=d, sl=sl: lambda y: y.dma_start(
                        d8[:, sl * FI:(sl + 1) * FI],
                        bass.AP(wd8_h.tensor, d * FI * 128,
                                [[128, 128], [128 * 128, NIC], [1, 128]])))(),
                        waits=w, inc=f"gd{sl}", inc_n=16)
                    gd_c[d] = c[f"gd{sl}"]
                    c = sch.emit("v", (lambda sl=sl: lambda v: v.tensor_copy(
                        d8[:, sl * FI:(sl + 1) * FI].bitcast(mybir.dt.uint32),
                        d8[:, sl * FI:(sl + 1) * FI].bitcast(mybir.dt.uint32)))(),
                        waits={f"gd{sl}": gd_c[d]})
                    q_d[d] = c["s_v"]

                    def dn_fn(d=d, sl=sl):
                        def fn(t):
                            last = None
                            for k in range(NIC):
                                for sub in range(2):
                                    last = t.matmul(
                                        pp[d % 2][:, sub * 512:(sub + 1) * 512],
                                        d8[:, sl * FI + k * 128:
                                           sl * FI + (k + 1) * 128],
                                        fuse[:, k * TH + sub * 512:
                                             k * TH + (sub + 1) * 512],
                                        start=(k == 0), stop=(k == NIC - 1))
                            return last
                        return fn
                    w = {f"gd{sl}": gd_c[d], "s_v": q_d[d]}
                    if d < 2:
                        w.update({k: max(w.get(k, 0), v) for k, v in pgE.items()})
                    if d >= 2:
                        w["s_s"] = s_ff[d - 2]
                    c = sch.emit("t", dn_fn(), waits=w, inc="s_t")
                    t_d[d] = c["s_t"]
                    c = sch.emit("s", (lambda d=d: lambda s: s.activation(
                        ffb[:, d * TH:(d + 1) * TH], pp[d % 2][:, :],
                        AF.Identity, scale=dsc[:, d:d + 1]))(),
                        waits={"s_t": t_d[d]}, inc="s_s")
                    s_ff[d] = c["s_s"]
                    w = {"s_s": s_ff[d]}
                    if d >= 2:
                        w["s_t"] = t_st.get(d - 2) or t_st[d - 3]
                    c = sch.emit("s", (lambda d=d, sl=sl: lambda s: s.activation(
                        sqrot[:, sl * TH:(sl + 1) * TH],
                        ffb[:, d * TH:(d + 1) * TH], AF.Square))(),
                        waits=w, inc="s_s")
                    s_sq[d] = c["s_s"]

                    def stat_fn(d=d, sl=sl):
                        def fn(t):
                            last = None
                            for sub in range(2):
                                last = t.matmul(
                                    pp[2][:, sub * 512:(sub + 1) * 512],
                                    ones_bf[:, :],
                                    ffb[:, d * TH + sub * 512:
                                        d * TH + (sub + 1) * 512],
                                    start=(d == 0), stop=(d == NDC - 1))
                            for sub in range(2):
                                last = t.matmul(
                                    pp[3][:, sub * 512:(sub + 1) * 512],
                                    ones_bf[:, :],
                                    sqrot[:, sl * TH + sub * 512:
                                          sl * TH + (sub + 1) * 512],
                                    start=(d == 0), stop=(d == NDC - 1))
                            return last
                        return fn
                    stat_pendE.append((d, stat_fn()))
                    if d >= 1:
                        pd, pfn = stat_pendE.pop(0)
                        c = sch.emit("t", pfn,
                                     waits={"s_s": s_sq[pd]}, inc="s_t")
                        t_st[pd] = c["s_t"]

                pd, pfn = stat_pendE.pop(0)
                c = sch.emit("t", pfn, waits={"s_s": s_sq[pd]}, inc="s_t")
                t_st[pd] = c["s_t"]
                sch.emit("s", lambda s: _load_act_set(
                    s, nc, "natural_log_exp_and_others"),
                    waits={"s_t": t_st[NDC - 1]})
                v_st3 = stats_math((m_b, va_b, vb_b, r_b, mr_b), pp[2], pp[3], t_st[NDC - 1])
                v_st3_prev = v_st3

                v_o = {}; y_o = {}
                for d in range(NDC):
                    sl = d % 4
                    w = dict(pgE) if d < 4 else {"s_v": v_o[d - 4]}
                    c = sch.emit("y", (lambda d=d, sl=sl: lambda y: y.dma_start(
                        x2rot[:, sl * TH:(sl + 1) * TH],
                        x2scr_h[d * 128:(d + 1) * 128, t0:t0 + TH]))(),
                        waits=w, inc=f"yr{sl}", inc_n=16)
                    yx = c[f"yr{sl}"]

                    def fin_fn(d=d, sl=sl):
                        def fn(v):
                            v.tensor_mul(sqrot[:, (d % 2) * TH:(d % 2 + 1) * TH],
                                         ffb[:, d * TH:(d + 1) * TH],
                                         r_b[:, :])
                            v.tensor_sub(sqrot[:, (d % 2) * TH:(d % 2 + 1) * TH],
                                         sqrot[:, (d % 2) * TH:(d % 2 + 1) * TH],
                                         mr_b[:, :])
                            return v.tensor_add(
                                outst[:, (d % 2) * TH:(d % 2 + 1) * TH],
                                sqrot[:, (d % 2) * TH:(d % 2 + 1) * TH],
                                x2rot[:, sl * TH:(sl + 1) * TH])
                        return fn
                    w = {f"yr{sl}": yx, "s_v": v_st3}
                    if d >= 2:
                        w[f"yo{d % 2}"] = y_o[d - 2]
                        w["s_v"] = max(w["s_v"], v_o[d - 2])
                    c = sch.emit("v", fin_fn(), waits=w, n=3)
                    v_o[d] = c["s_v"]
                    c = sch.emit("y" if d % 2 == 0 else "g",
                                 (lambda d=d: lambda q: q.dma_start(
                                     out_h[d * 128:(d + 1) * 128, t0:t0 + TH],
                                     outst[:, (d % 2) * TH:(d % 2 + 1) * TH]))(),
                                 waits={"s_v": v_o[d]}, inc=f"yo{d % 2}",
                                 inc_n=16)
                    y_o[d] = c[f"yo{d % 2}"]
                sch.barrier(exclude=("g", "y", "t", "s"))
                sch.flush()
            fuse_st.close()
            if _done():
                break

        sch.emit("y", None, waits={"yo0": sch.cnt["yo0"], "yo1": sch.cnt["yo1"]})
        sch.flush()
    return nc


def _prep_inputs(kw):
    x = np.asarray(kw["L0_attn_hidden_states"], dtype=np.float32)
    in_w = np.asarray(kw["L0_attn_in_proj_w"], dtype=np.float32)
    gt_w = np.asarray(kw["L0_attn_gate_proj_w"], dtype=np.float32)
    out_w = np.asarray(kw["L0_attn_out_proj_w"], dtype=np.float32)
    conv_w = np.asarray(kw["L0_attn_conv_w"], dtype=np.float32)
    conv_b = np.asarray(kw["L0_attn_conv_b"], dtype=np.float32)
    dt_w = np.asarray(kw["L0_attn_dt_w"], dtype=np.float32)
    dt_b = np.asarray(kw["L0_attn_dt_b"], dtype=np.float32)
    A_log = np.asarray(kw["L0_attn_A_log"], dtype=np.float32)
    Bp = np.asarray(kw["L0_attn_Bp"], dtype=np.float32)
    Cp = np.asarray(kw["L0_attn_Cp"], dtype=np.float32)
    Dp = np.asarray(kw["L0_attn_Dp"], dtype=np.float32)
    gate_w = np.asarray(kw["L0_ffn_gate_w"], dtype=np.float32)
    up_w = np.asarray(kw["L0_ffn_up_w"], dtype=np.float32)
    down_w = np.asarray(kw["L0_ffn_down_w"], dtype=np.float32)
    gate_sc = np.asarray(kw["L0_ffn_gate_scale"], dtype=np.float32)
    up_sc = np.asarray(kw["L0_ffn_up_scale"], dtype=np.float32)
    down_sc = np.asarray(kw["L0_ffn_down_scale"], dtype=np.float32)

    win = np.ascontiguousarray(in_w.reshape(NCC, 128, D).transpose(0, 2, 1))
    wgt = np.ascontiguousarray(gt_w.reshape(NCC, 128, D).transpose(0, 2, 1))
    wout = np.ascontiguousarray(out_w.reshape(NDC, 128, INNER).transpose(0, 2, 1))
    wg = np.ascontiguousarray(gate_w.reshape(D, NIC, 128).transpose(1, 0, 2))
    wu = np.ascontiguousarray(up_w.reshape(D, NIC, 128).transpose(1, 0, 2))
    wd = np.ascontiguousarray(down_w.reshape(FI, NDC, 128).transpose(1, 0, 2))

    dcv = np.zeros((NCC, 4, 128, 128), dtype=np.float32)
    idx = np.arange(128)
    dcv[:, :, idx, idx] = conv_w.reshape(NCC, 128, 4).transpose(0, 2, 1)

    dtw = np.zeros((NCC, 128, 128), dtype=np.float32)
    for cc in range(NCC):
        for hh in range(2):
            hd = 2 * cc + hh
            blk = np.broadcast_to(dt_w[hd][:, None], (64, 64))
            dtw[cc, hh * 64:(hh + 1) * 64, hh * 64:(hh + 1) * 64] = blk

    chan = np.zeros((128, 320), dtype=np.float32)

    def per_chan(vec):
        return vec.reshape(NCC, 128).T
    chan[:, 0:32] = per_chan(A_log.reshape(-1))
    chan[:, 32:64] = per_chan(Bp.reshape(-1))
    chan[:, 64:96] = per_chan(Cp.reshape(-1))
    chan[:, 96:128] = per_chan(Dp.reshape(-1))
    chan[:, 128:160] = per_chan(np.repeat(dt_b, 64))
    chan[:, 160:192] = per_chan(conv_b)
    for k in range(4):
        chan[:, 192 + k * 32:192 + (k + 1) * 32] = per_chan(conv_w[:, k])

    gsc = np.ascontiguousarray(gate_sc.reshape(NIC, 128).T)
    usc = np.ascontiguousarray(up_sc.reshape(NIC, 128).T)
    dsc = np.ascontiguousarray(down_sc.reshape(NDC, 128).T)

    shared = dict(win=win, wgt=wgt, wout=wout, wg=wg, wu=wu, wd=wd,
                  dtw=dtw, dcv=dcv, chanv=chan, gsc=gsc, usc=usc, dsc=dsc)
    xT = np.ascontiguousarray(x.transpose(0, 2, 1))
    in_maps = [dict(shared, x=xT[b]) for b in range(x.shape[0])]
    return in_maps, x.shape


def kernel(**kw):
    in_maps, xshape = _prep_inputs(kw)
    nc = build_nc()
    res = run_bass_kernel_spmd(nc, in_maps, core_ids=list(range(len(in_maps))))
    B = xshape[0]
    out = np.empty(xshape, dtype=np.float32)
    for b in range(B):
        out[b] = res.results[b]["out"].T
    return out


# version 18
# speedup vs baseline: 1.0507x; 1.0507x over previous
import numpy as np
from contextlib import ExitStack

import concourse.bass as bass
import concourse.mybir as mybir
from concourse.bass_utils import run_bass_kernel_spmd
from concourse.hw_specs import get_activation_tables

F32 = mybir.dt.float32
BF16 = mybir.dt.bfloat16
FP8 = mybir.dt.float8e4
AF = mybir.ActivationFunctionType
ALU = mybir.AluOpType

D = 2048
L = 2048
INNER = 4096
FI = 6144
NDC = D // 128
NCC = INNER // 128
NIC = FI // 128
TH = 1024
NH = 2
EPS = 1e-5

ENGINES = ("g", "t", "s", "v", "y")
DBG_B_CONV = True
DBG_B_LOOP2 = True
DBG_B_ACTS = True
DBG_B_NB = 8
DBG_B_MM_ONLY = False
DBG_B_NOSLAB = False
DBG_C_LEVEL = 9
DBG_CONV_DEFER = 2

DMA_SEMS = ("gcn", "gx0", "gx1", "gx2", "gx3", "yr2", "yr3", "gi0", "gi1", "gg0", "gg1", "go0", "go1",
            "gdc0", "gdc1", "gq8", "gb0", "gb1", "gb2", "gb3", "gu2", "gu3", "gu8", "gu0", "gu1", "gd8", "gd0", "gd1",
            "ys0", "ys1", "yr0", "yr1", "yo0", "yo1")
CMP_SEMS = ("s_t", "s_s", "s_v", "gch")


class Sched:

    def __init__(self, nc, sems):
        self.nc = nc
        self.sems = sems
        self.cnt = {k: 0 for k in sems}
        self.ecnt = {"s_v": 0, "s_s": 0}
        self.q = {e: [] for e in ENGINES}

    ENG_SEM = {"v": "s_v", "s": "s_s"}

    def emit(self, eng, fn=None, waits=None, inc=None, inc_n=1, n=1):
        waits = {k: v for k, v in (waits or {}).items() if v > 0}
        for snm, v in waits.items():
            assert v <= self.cnt[snm], (
                f"wait on {snm}>={v} but only {self.cnt[snm]} scheduled")
        if eng in self.ENG_SEM:
            self.q[eng].append((waits, fn, None, 0))
            if fn is not None:
                self.cnt[self.ENG_SEM[eng]] += n
        else:
            self.q[eng].append((waits, fn, inc, inc_n))
            if inc is not None:
                self.cnt[inc] += inc_n
        return dict(self.cnt)

    def barrier(self, exclude=()):
        snap = dict(self.cnt)
        for e in ENGINES:
            if e not in exclude:
                self.emit(e, None, waits=snap)

    def guard(self):
        return {k: self.cnt[k] for k in ("s_t", "s_s", "s_v")}

    def flush(self):
        nc = self.nc
        qs = self.q
        self.q = {e: [] for e in ENGINES}
        sems = self.sems

        def mk(entries, chain_sem):
            def run(eng):
                if chain_sem is None:
                    for waits, fn, inc, inc_n in entries:
                        for snm, v in waits.items():
                            eng.wait_ge(sems[snm], v)
                        if fn is not None:
                            instr = fn(eng)
                            if inc is not None:
                                instr.then_inc(sems[inc], inc_n)
                    return
                proxy = _ChainProxy(eng, sems[chain_sem], self.ecnt, chain_sem)
                for waits, fn, inc, inc_n in entries:
                    for snm, v in waits.items():
                        eng.wait_ge(sems[snm], v)
                    if fn is not None:
                        fn(proxy)
            return run

        with nc.Block() as block:
            if qs["g"]:
                block.gpsimd(mk(qs["g"], None))
            if qs["t"]:
                block.tensor(mk(qs["t"], None))
            if qs["s"]:
                block.scalar(mk(qs["s"], "s_s"))
            if qs["v"]:
                block.vector(mk(qs["v"], "s_v"))
            if qs["y"]:
                block.sync(mk(qs["y"], None))
        for e, snm in self.ENG_SEM.items():
            assert self.ecnt[snm] == self.cnt[snm], (
                f"{snm}: emitted {self.ecnt[snm]} != scheduled {self.cnt[snm]}"
                " (check n= instruction counts)")


class _ChainProxy:
    def __init__(self, eng, sem, ecnt, key):
        self._eng = eng
        self._sem = sem
        self._ecnt = ecnt
        self._key = key

    def __getattr__(self, nm):
        f = getattr(self._eng, nm)
        if nm in ("wait_ge", "drain"):
            return f

        def g(*a, **k):
            n = self._ecnt[self._key]
            if n > 0:
                self._eng.wait_ge(self._sem, n)
            instr = f(*a, **k)
            instr.then_inc(self._sem, 1)
            self._ecnt[self._key] = n + 1
            return instr
        return g


def _load_act_set(eng, nc, set_name):
    tabs = list(get_activation_tables(nc.m.arch).keys())
    idx = tabs.index(set_name)
    return eng.add_instruction(mybir.InstLoadActFuncSet(
        name=nc.get_next_instruction_name(), act_func_set_id=idx, ins=[], outs=[]))


def build_nc(limit=99):
    nc = bass.Bass("TRN2", target_bir_lowering=False)

    x_h = nc.dram_tensor("x", [D, L], F32, kind="ExternalInput").ap()
    win_h = nc.dram_tensor("win", [NCC, D, 128], F32, kind="ExternalInput").ap()
    wgt_h = nc.dram_tensor("wgt", [NCC, D, 128], F32, kind="ExternalInput").ap()
    wout_h = nc.dram_tensor("wout", [NDC, INNER, 128], F32, kind="ExternalInput").ap()
    wg_h = nc.dram_tensor("wg", [NIC, D, 128], F32, kind="ExternalInput").ap()
    wu_h = nc.dram_tensor("wu", [NIC, D, 128], F32, kind="ExternalInput").ap()
    wd_h = nc.dram_tensor("wd", [NDC, FI, 128], F32, kind="ExternalInput").ap()
    dtw_h = nc.dram_tensor("dtw", [NCC, 128, 128], F32, kind="ExternalInput").ap()
    dcv_h = nc.dram_tensor("dcv", [NCC, 4, 128, 128], F32, kind="ExternalInput").ap()
    chanv_h = nc.dram_tensor("chanv", [128, 320], F32, kind="ExternalInput").ap()
    gsc_h = nc.dram_tensor("gsc", [128, NIC], F32, kind="ExternalInput").ap()
    usc_h = nc.dram_tensor("usc", [128, NIC], F32, kind="ExternalInput").ap()
    dsc_h = nc.dram_tensor("dsc", [128, NDC], F32, kind="ExternalInput").ap()
    out_h = nc.dram_tensor("out", [D, L], F32, kind="ExternalOutput").ap()
    x2scr_h = nc.dram_tensor("x2scr", [D, L], BF16, kind="Internal").ap()
    wg8_h = nc.dram_tensor("wg8s", [NIC, D, 128], FP8, kind="Internal").ap()
    wu8_h = nc.dram_tensor("wu8s", [NIC, D, 128], FP8, kind="Internal").ap()
    wd8_h = nc.dram_tensor("wd8s", [NDC, FI, 128], FP8, kind="Internal").ap()

    st = ExitStack()
    S_ = st.enter_context
    with st:
        sems = {}
        for nm in DMA_SEMS + CMP_SEMS:
            sems[nm] = S_(nc.semaphore(nm))
        sch = Sched(nc, sems)

        chanv = S_(nc.sbuf_tensor("chanv_s", [128, 320], F32, side="right"))
        gsc = S_(nc.sbuf_tensor("gsc_s", [128, NIC], F32, side="right"))
        usc = S_(nc.sbuf_tensor("usc_s", [128, NIC], F32, side="right"))
        dsc = S_(nc.sbuf_tensor("dsc_s", [128, NDC], F32, side="right"))
        dtwst = S_(nc.sbuf_tensor("dtwst", [128, NCC * 128], BF16, side="right"))
        negA = S_(nc.sbuf_tensor("negA", [128, NCC], F32, side="right"))
        Cp2 = S_(nc.sbuf_tensor("Cp2", [128, NCC], F32, side="right"))
        Dp2 = S_(nc.sbuf_tensor("Dp2", [128, NCC], F32, side="right"))
        ones_bf = S_(nc.sbuf_tensor("ones_bf", [128, 128], BF16, side="right"))
        carry = S_(nc.sbuf_tensor("carry", [128, NCC], F32, side="right"))
        tails = S_(nc.sbuf_tensor("tails", [128, NCC * 3], BF16, side="right"))
        etmp = S_(nc.sbuf_tensor("etmp", [128, NCC], F32, side="right"))
        cst = S_(nc.sbuf_tensor("cst", [128, 2], F32, side="right"))
        xrot = S_(nc.sbuf_tensor("xrot_p", [128, 4 * TH], BF16, side="right"))
        xsq = S_(nc.sbuf_tensor("xsq_p", [128, 2 * TH], BF16, side="right"))
        nc.const_aps.register(F32, 0.5, cst[:, 0:1])
        nc.const_aps.register(F32, -0.5, cst[:, 1:2])

        pp = [S_(nc.psum_tensor(f"pp{j}", [128, 1024], F32)) for j in range(4)]

        def cv(group, cc):
            return chanv[:, group * 32 + cc: group * 32 + cc + 1]

        sch.emit("g", lambda g: g.dma_start(chanv[:, :], chanv_h),
                 inc="gcn", inc_n=16)
        sch.emit("g", lambda g: g.dma_start(gsc[:, :], gsc_h), inc="gcn", inc_n=16)
        sch.emit("g", lambda g: g.dma_start(usc[:, :], usc_h), inc="gcn", inc_n=16)
        sch.emit("g", lambda g: g.dma_start(dsc[:, :], dsc_h), inc="gcn", inc_n=16)
        sch.emit("g", lambda g: g.dma_start(
            dtwst[:, :],
            bass.AP(dtw_h.tensor, 0, [[128, 128], [128 * 128, NCC], [1, 128]])),
            inc="gcn", inc_n=16)
        gd_const = sch.cnt["gcn"]
        sch.emit("v", lambda v: v.memset(ones_bf[:, :], 1.0), inc="s_v")
        sch.emit("v", lambda v: v.memset(cst[:, 0:1], 0.5), inc="s_v")
        sch.emit("v", lambda v: v.memset(cst[:, 1:2], -0.5), inc="s_v")
        sch.emit("v", lambda v: v.memset(carry[:, :], 0.0), inc="s_v")
        sch.emit("v", lambda v: v.memset(tails[:, :], 0.0), inc="s_v")
        sch.emit("s", lambda s: _load_act_set(s, nc, "natural_log_exp_and_others"),
                 waits={"gcn": gd_const})
        sch.emit("s", lambda s: s.activation(etmp[:, :], chanv[:, 0:32], AF.Exp),
                 inc="s_s")
        s_after_exp = sch.cnt["s_s"]
        sch.emit("v", lambda v: v.tensor_scalar_mul(negA[:, :], etmp[:, :], -1.0),
                 waits={"s_s": s_after_exp}, inc="s_v")
        sch.emit("v", lambda v: v.tensor_scalar_mul(Cp2[:, :], chanv[:, 64:96], 0.5),
                 inc="s_v")
        sch.emit("v", lambda v: v.tensor_scalar_mul(Dp2[:, :], chanv[:, 96:128], 0.5),
                 inc="s_v")
        sch.barrier()
        sch.flush()
        _flushed = [1]

        def _done():
            _flushed[0] += 1
            return _flushed[0] > limit

        def stats_math(bufs, p_sum, p_sq, t_rdy):
            m_b, va_b, vb_b, r_b, mr_b = bufs
            sch.emit("v", lambda v: v.tensor_scalar_mul(
                m_b[:, :], p_sum[:, :], 1.0 / D),
                waits={"s_t": t_rdy}, inc="s_v")
            sch.emit("v", lambda v: v.tensor_scalar_mul(
                va_b[:, :], p_sq[:, :], 1.0 / D), inc="s_v")
            sch.emit("v", lambda v: v.tensor_mul(
                vb_b[:, :], m_b[:, :], m_b[:, :]), inc="s_v")
            c = sch.emit("v", lambda v: v.scalar_tensor_tensor(
                va_b[:, :], va_b[:, :], EPS, vb_b[:, :],
                ALU.add, ALU.subtract), inc="s_v")
            sch.emit("s", lambda s: s.activation(
                vb_b[:, :], va_b[:, :], AF.Ln),
                waits={"s_v": c["s_v"]}, inc="s_s")
            c = sch.emit("s", lambda s: s.activation(
                r_b[:, :], vb_b[:, :], AF.Exp, scale=-0.5), inc="s_s")
            c = sch.emit("v", lambda v: v.tensor_mul(
                mr_b[:, :], m_b[:, :], r_b[:, :]),
                waits={"s_s": c["s_s"]}, inc="s_v")
            return c["s_v"]

        v_st3_prev = None
        for h in range(NH):
            t0 = h * TH

            comb_st = ExitStack()
            combined = comb_st.enter_context(
                nc.sbuf_tensor(f"comb{h}", [128, NCC * TH], BF16, side="right"))
            ab_st = ExitStack()
            ln1 = ab_st.enter_context(
                nc.sbuf_tensor(f"ln1_{h}", [128, NDC * TH], BF16))

            pgA = pg_xrot if h > 0 else sch.guard()
            with ExitStack() as a_st:
                AS = a_st.enter_context
                m_b = AS(nc.sbuf_tensor(f"m_b{h}", [128, TH], F32))
                va_b = AS(nc.sbuf_tensor(f"va_b{h}", [128, TH], F32))
                vb_b = AS(nc.sbuf_tensor(f"vb_b{h}", [128, TH], F32))
                r_b = AS(nc.sbuf_tensor(f"r_b{h}", [128, TH], BF16))
                mr_b = AS(nc.sbuf_tensor(f"mr_b{h}", [128, TH], BF16))
                xf32 = AS(nc.sbuf_tensor(f"xf32_{h}", [128, 4 * TH], F32))

                sch.emit("s", lambda s: _load_act_set(
                    s, nc, "natural_log_exp_and_others"))
                gcum = {}
                scnt = {}
                tcnt = {}
                for d in range(NDC):
                    sl = d % 4
                    w = dict(pgA) if d < 4 else {"s_t": tcnt[d - 4]}
                    c = sch.emit("g", (lambda d=d, sl=sl: lambda g: g.dma_start(
                        xrot[:, sl * TH:(sl + 1) * TH],
                        x_h[d * 128:(d + 1) * 128, t0:t0 + TH]))(),
                        waits=w, inc=f"gx{sl}", inc_n=16)
                    gcum[d] = c[f"gx{sl}"]
                    c = sch.emit("s", (lambda d=d, sl=sl: lambda s: s.activation(
                        xsq[:, (d % 2) * TH:(d % 2 + 1) * TH],
                        xrot[:, sl * TH:(sl + 1) * TH], AF.Square))(),
                        waits={f"gx{sl}": gcum[d],
                               **({"s_t": tcnt[d - 2]} if d >= 2 else {})},
                        inc="s_s")
                    scnt[d] = c["s_s"]

                    def stats_fn(d=d, sl=sl):
                        def fn(t):
                            last = None
                            for sub in range(2):
                                last = t.matmul(
                                    pp[2][:, sub * 512:(sub + 1) * 512],
                                    ones_bf[:, :],
                                    xrot[:, sl * TH + sub * 512:
                                         sl * TH + (sub + 1) * 512],
                                    start=(d == 0), stop=(d == NDC - 1))
                            for sub in range(2):
                                last = t.matmul(
                                    pp[3][:, sub * 512:(sub + 1) * 512],
                                    ones_bf[:, :],
                                    xsq[:, (d % 2) * TH + sub * 512:
                                        (d % 2) * TH + (sub + 1) * 512],
                                    start=(d == 0), stop=(d == NDC - 1))
                            return last
                        return fn
                    w = {f"gx{sl}": gcum[d], "s_s": scnt[d]}
                    if d == 0 and v_st3_prev is not None:
                        w["s_v"] = v_st3_prev
                    c = sch.emit("t", stats_fn(), waits=w, inc="s_t")
                    tcnt[d] = c["s_t"]

                v_stats = stats_math((m_b, va_b, vb_b, r_b, mr_b), pp[2], pp[3], tcnt[NDC - 1])

                vap = {}
                for d in range(NDC):
                    sl = d % 4
                    w = {} if d < 4 else {"s_v": vap[d - 4]}
                    c = sch.emit("y", (lambda d=d, sl=sl: lambda y: y.dma_start(
                        xf32[:, sl * TH:(sl + 1) * TH],
                        x_h[d * 128:(d + 1) * 128, t0:t0 + TH]))(),
                        waits=w, inc=f"yr{sl}", inc_n=16)
                    yx = c[f"yr{sl}"]

                    def apply_fn(d=d, sl=sl):
                        def fn(v):
                            v.tensor_mul(
                                xsq[:, (d % 2) * TH:(d % 2 + 1) * TH],
                                xf32[:, sl * TH:(sl + 1) * TH], r_b[:, :])
                            return v.tensor_sub(
                                ln1[:, d * TH:(d + 1) * TH],
                                xsq[:, (d % 2) * TH:(d % 2 + 1) * TH], mr_b[:, :])
                        return fn
                    c = sch.emit("v", apply_fn(),
                                 waits={f"yr{sl}": yx,
                                        **({"s_v": v_stats} if d < 2 else {})},
                                 n=2)
                    vap[d] = c["s_v"]
                sch.barrier(exclude=("g", "y", "t", "s"))
                sch.flush()
            if _done():
                ab_st.close(); comb_st.close(); break

            pgB = sch.guard()
            with ExitStack() as b_st:
                BS = b_st.enter_context
                inslab = BS(nc.sbuf_tensor(f"inslab{h}", [128, 2 * D], BF16))
                gslab = BS(nc.sbuf_tensor(f"gslab{h}", [128, 2 * D], BF16))
                cbuf = BS(nc.sbuf_tensor(f"cbuf{h}", [128, 3 * TH], F32))
                gth = BS(nc.sbuf_tensor(f"gth{h}", [128, 4 * TH], BF16))
                Xb = BS(nc.sbuf_tensor(f"Xb{h}", [128, 4 * TH], BF16))
                ebuf = BS(nc.sbuf_tensor(f"ebuf{h}", [128, 4 * TH], F32))
                dtbuf = BS(nc.sbuf_tensor(f"dtbuf{h}", [128, 2 * TH], F32))
                dkbuf = BS(nc.sbuf_tensor(f"dkbuf{h}", [128, 2 * TH], F32))
                drv = BS(nc.sbuf_tensor(f"drv{h}", [128, TH], F32))
                yAb = BS(nc.sbuf_tensor(f"yAb{h}", [128, 2 * TH], F32))

                t_in = {}; t_gate = {}; t_dt = {}; t_cv = {}
                conv_pend = []
                v_conv = {}; v_drv = {}; v_scan = {}; v_t = {}; v_comb = {}
                s_gth = {}; s_X = {}; s_e = {}; s_dk = {}; s_yA = {}
                g_in = {}; g_gt = {}

                def slab_dma(dst, slot, src3d, cc, nk):
                    def fn(g):
                        return g.dma_start(
                            dst[:, slot * nk * 128:(slot + 1) * nk * 128],
                            bass.AP(src3d.tensor,
                                    cc * nk * 128 * 128,
                                    [[128, 128], [128 * 128, nk], [1, 128]]))
                    return fn

                g_dc = {}

                def sched_slab(cc):
                    sl = cc % 2
                    w = dict(pgB) if cc < 2 else {"s_t": t_in[cc - 2]}
                    c = sch.emit("g", slab_dma(inslab, sl, win_h, cc, NDC),
                                 waits=w, inc=f"gi{sl}", inc_n=16)
                    g_in[cc] = c[f"gi{sl}"]
                    w = dict(pgB) if cc < 2 else {"s_t": t_gate[cc - 2]}
                    c = sch.emit("g", slab_dma(gslab, sl, wgt_h, cc, NDC),
                                 waits=w, inc=f"gg{sl}", inc_n=16)
                    g_gt[cc] = c[f"gg{sl}"]

                if not DBG_B_NOSLAB:
                    sched_slab(0)
                    sched_slab(1)

                def emit_dt(cc):
                    j = cc % 4

                    def dt_fn(cc=cc, j=j):
                        def fn(t):
                            last = None
                            for sub in range(2):
                                last = t.matmul(
                                    pp[cc % 2][:, sub * 512:(sub + 1) * 512],
                                    dtwst[:, cc * 128:(cc + 1) * 128],
                                    Xb[:, j * TH + sub * 512:
                                       j * TH + (sub + 1) * 512],
                                    start=True, stop=True)
                            return last
                        return fn
                    pair_rd = v_conv[cc + 2] if j < 2 else v_conv[cc]
                    w = {"s_s": s_X[cc], "s_v": pair_rd}
                    if cc >= 2:
                        w["s_s"] = max(w["s_s"], s_e[cc - 2])
                    c = sch.emit("t", dt_fn(), waits=w, inc="s_t")
                    t_dt[cc] = c["s_t"]

                def emit_silu(cc):
                    j = cc % 4
                    w = {"s_v": v_conv[cc]}
                    if cc >= 4:
                        w["s_t"] = t_dt[cc - 4]
                        w["s_v"] = max(w["s_v"], v_t[cc - 4])
                    c = sch.emit("s", (lambda cc=cc, j=j: lambda s:
                                  s.activation(Xb[:, j * TH:(j + 1) * TH],
                                               cbuf[:, (cc % 3) * TH:
                                                    (cc % 3 + 1) * TH],
                                               AF.Silu))(),
                                 waits=w, inc="s_s")
                    s_X[cc] = c["s_s"]

                def emit_e(cc):
                    j = cc % 4
                    w = {"s_t": t_dt[cc]}
                    c = sch.emit("s", (lambda cc=cc, j=j: lambda s: s.activation(
                        ebuf[:, j * TH:(j + 1) * TH], pp[cc % 2][:, :], AF.Exp,
                        bias=cv(4, cc)))(), waits=w, inc="s_s")
                    s_e[cc] = c["s_s"]

                def mm_group(pair, slab, slot, nk, mov):
                    def fn(t):
                        last = None
                        for k in range(nk):
                            for sub in range(2):
                                last = t.matmul(
                                    pp[pair][:, sub * 512:(sub + 1) * 512],
                                    slab[:, slot * nk * 128 + k * 128:
                                         slot * nk * 128 + (k + 1) * 128],
                                    mov[:, k * TH + sub * 512:
                                        k * TH + (sub + 1) * 512],
                                    start=(k == 0), stop=(k == nk - 1))
                        return last
                    return fn

                for b in range(DBG_B_NB):
                    sch.emit("s", lambda s: _load_act_set(s, nc, "silu_and_others"))
                    for j in range(4):
                        cc = b * 4 + j
                        if h == 0 and cc % 4 == 0:
                            ci = cc // 4
                            sch.emit("g", (lambda ci=ci: lambda g: g.dma_start(
                                bass.AP(wg8_h.tensor, ci * 6 * D * 128,
                                        [[1, 1], [1, 6 * D * 128]]),
                                bass.AP(wg_h.tensor, ci * 6 * D * 128,
                                        [[1, 1], [1, 6 * D * 128]])))(),
                                inc="gq8", inc_n=16)
                        if h == 0 and cc % 4 == 2:
                            ci = cc // 4
                            sch.emit("g", (lambda ci=ci: lambda g: g.dma_start(
                                bass.AP(wu8_h.tensor, ci * 6 * D * 128,
                                        [[1, 1], [1, 6 * D * 128]]),
                                bass.AP(wu_h.tensor, ci * 6 * D * 128,
                                        [[1, 1], [1, 6 * D * 128]])))(),
                                inc="gu8", inc_n=16)
                        w = {f"gi{cc % 2}": g_in[cc]}
                        if cc < 2:
                            w.update(pgB)
                        if cc >= 2:
                            w["s_v"] = v_conv.get(cc - 2, 0)
                            w["s_s"] = (s_e[cc - 2] if j < 2
                                        else s_X[cc - 2]) if not DBG_B_MM_ONLY else 0
                        c = sch.emit("t", mm_group(cc % 2, inslab, cc % 2,
                                                   NDC, ln1),
                                     waits=w, inc="s_t")
                        t_in[cc] = c["s_t"]
                        w = {f"gg{cc % 2}": g_gt[cc]}
                        if cc < 2:
                            w.update(pgB)
                        if cc >= 2:
                            w["s_s"] = s_gth.get(cc - 2, 0)
                        c = sch.emit("t", mm_group(2 + cc % 2, gslab, cc % 2,
                                                   NDC, ln1),
                                     waits=w, inc="s_t")
                        t_gate[cc] = c["s_t"]
                        if DBG_B_MM_ONLY:
                            if cc + 2 < NCC:
                                sched_slab(cc + 2)
                            continue
                        def conv_fn(cc=cc, j=cc % 3):
                            pj = pp[cc % 2]
                            cs = cbuf[:, j * TH:(j + 1) * TH]
                            def fn(v):
                                v.tensor_scalar(cs, pj[:, :],
                                                cv(9, cc), cv(5, cc),
                                                ALU.mult, ALU.add)
                                for k in range(3):
                                    sh = 3 - k
                                    v.scalar_tensor_tensor(
                                        cbuf[:, j * TH + sh:(j + 1) * TH],
                                        pj[:, 0:TH - sh], cv(6 + k, cc),
                                        cbuf[:, j * TH + sh:(j + 1) * TH],
                                        ALU.mult, ALU.add)
                                    v.scalar_tensor_tensor(
                                        cbuf[:, j * TH:j * TH + sh],
                                        tails[:, cc * 3 + k:cc * 3 + 3],
                                        cv(6 + k, cc),
                                        cbuf[:, j * TH:j * TH + sh],
                                        ALU.mult, ALU.add)
                                return v.tensor_copy(
                                    tails[:, cc * 3:cc * 3 + 3],
                                    pj[:, TH - 3:TH])
                            return fn
                        w = {"s_t": t_in[cc]}
                        if cc >= 3:
                            w["s_s"] = s_X[cc - 3]
                        c = sch.emit("v", conv_fn(), waits=w, n=8)
                        v_conv[cc] = c["s_v"]
                        t_cv[cc] = sch.cnt["s_t"]
                        emit_silu(cc)
                        if 1 <= cc < NCC - 1:
                            sched_slab(cc + 1)
                        w = {"s_t": t_gate[cc]}
                        if cc >= 4:
                            w["s_v"] = v_comb[cc - 4]
                        c = sch.emit("s", (lambda cc=cc, j=j: lambda s:
                                      s.activation(gth[:, j * TH:(j + 1) * TH],
                                                   pp[2 + cc % 2][:, :],
                                                   AF.Tanh, scale=0.5))(),
                                     waits=w, inc="s_s")
                        s_gth[cc] = c["s_s"]


                    if DBG_B_MM_ONLY:
                        continue
                    if not DBG_B_LOOP2:
                        for j in range(4):
                            cc = b * 4 + j
                            c = sch.emit("v", (lambda cc=cc, j=j: lambda v:
                                          v.tensor_copy(
                                              combined[:, cc * TH:(cc + 1) * TH],
                                              Xb[:, j * TH:(j + 1) * TH]))(),
                                         waits={"s_s": s_X[cc]}, inc="s_v")
                            v_comb[cc] = c["s_v"]
                            t_dt[cc] = sch.cnt["s_t"]
                            s_e[cc] = sch.cnt["s_s"]
                            v_t[cc] = c["s_v"]
                            v_scan[cc] = c["s_v"]
                            v_drv[cc] = c["s_v"]
                        continue
                    sch.emit("s", lambda s: _load_act_set(
                        s, nc, "natural_log_exp_and_others"))
                    for j in range(4):
                        cc = b * 4 + j
                        if j >= 2:
                            emit_e(cc - 2)
                        emit_dt(cc)
                    emit_e(b * 4 + 2)
                    emit_e(b * 4 + 3)
                    for j in range(4):
                        cc = b * 4 + j
                        w = {}
                        if cc >= 2:
                            w["s_v"] = v_drv[cc - 2]
                        c = sch.emit("s", (lambda cc=cc, j=j: lambda s:
                                      s.activation(
                                          dtbuf[:, (j % 2) * TH:(j % 2 + 1) * TH],
                                          ebuf[:, j * TH:(j + 1) * TH],
                                          AF.Ln, bias=1.0))(),
                            waits=w, inc="s_s")
                        w = {}
                        if cc >= 2:
                            w = {"s_v": v_scan[cc - 2]}
                        c = sch.emit("s", (lambda cc=cc, j=j: lambda s:
                                      s.activation(
                                          dkbuf[:, (cc % 2) * TH:(cc % 2 + 1) * TH],
                                          dtbuf[:, (j % 2) * TH:(j % 2 + 1) * TH],
                                          AF.Exp, scale=negA[:, cc:cc + 1]))(),
                            waits=w, inc="s_s")
                        s_dk[cc] = c["s_s"]
                        c = sch.emit("v", (lambda cc=cc, j=j: lambda v:
                                      v.scalar_tensor_tensor(
                                          drv[:, :],
                                          dtbuf[:, (j % 2) * TH:(j % 2 + 1) * TH],
                                          cv(1, cc),
                                          Xb[:, j * TH:(j + 1) * TH],
                                          ALU.mult, ALU.mult))(),
                                     waits={"s_s": s_dk[cc]}, inc="s_v")
                        v_drv[cc] = c["s_v"]
                        c = sch.emit("v", (lambda cc=cc: lambda v:
                                      v.tensor_tensor_scan(
                                          drv[:, :],
                                          dkbuf[:, (cc % 2) * TH:(cc % 2 + 1) * TH],
                                          drv[:, :],
                                          carry[:, cc:cc + 1],
                                          ALU.mult, ALU.add))(),
                                     inc="s_v")
                        if h == 0:
                            c = sch.emit("v", (lambda cc=cc: lambda v:
                                          v.tensor_copy(carry[:, cc:cc + 1],
                                                        drv[:, TH - 1:TH]))(),
                                         inc="s_v")
                        v_scan[cc] = c["s_v"]
                        w = {"s_v": v_scan[cc]}
                        if cc >= 2:
                            w["s_v"] = max(w["s_v"], v_t[cc - 2])
                        c = sch.emit("s", (lambda cc=cc: lambda s: s.activation(
                            yAb[:, (cc % 2) * TH:(cc % 2 + 1) * TH],
                            drv[:, :], AF.Identity,
                            scale=Cp2[:, cc:cc + 1]))(),
                            waits=w, inc="s_s")
                        s_yA[cc] = c["s_s"]
                        c = sch.emit("v", (lambda cc=cc, j=j: lambda v:
                                      v.scalar_tensor_tensor(
                                          yAb[:, (cc % 2) * TH:(cc % 2 + 1) * TH],
                                          Xb[:, j * TH:(j + 1) * TH],
                                          Dp2[:, cc:cc + 1],
                                          yAb[:, (cc % 2) * TH:(cc % 2 + 1) * TH],
                                          ALU.mult, ALU.add))(),
                                     waits={"s_s": s_yA[cc]}, inc="s_v")
                        v_t[cc] = c["s_v"]
                        c = sch.emit("v", (lambda cc=cc, j=j: lambda v:
                                      v.scalar_tensor_tensor(
                                          combined[:, cc * TH:(cc + 1) * TH],
                                          gth[:, j * TH:(j + 1) * TH], 1.0,
                                          yAb[:, (cc % 2) * TH:(cc % 2 + 1) * TH],
                                          ALU.add, ALU.mult))(),
                                     waits={"s_s": s_gth[cc]}, inc="s_v")
                        v_comb[cc] = c["s_v"]
                sch.barrier(exclude=("g", "y", "t", "s"))
                sch.flush()
            if _done():
                ab_st.close(); comb_st.close(); break

            pgC = sch.guard()
            ab_st.close()
            cd_st = ExitStack()
            ln2b = cd_st.enter_context(
                nc.sbuf_tensor(f"ln2b{h}", [128, NDC * TH], BF16))
            with ExitStack() as c_st:
                CS = c_st.enter_context
                attnb = CS(nc.sbuf_tensor(f"attnb{h}", [128, NDC * TH], BF16))
                xrot = xrot
                ost_st = ExitStack()
                ost = ost_st.enter_context(
                    nc.sbuf_tensor(f"ost{h}", [128, 2 * INNER], BF16,
                                   side="right"))
                sqrot = CS(nc.sbuf_tensor(f"sqrot{h}", [128, 2 * TH], BF16))
                sq2rot = CS(nc.sbuf_tensor(f"sq2rot{h}", [128, 2 * TH], BF16))
                m_b = CS(nc.sbuf_tensor(f"m_bC{h}", [128, TH], F32))
                va_b = CS(nc.sbuf_tensor(f"va_bC{h}", [128, TH], F32))
                vb_b = CS(nc.sbuf_tensor(f"vb_bC{h}", [128, TH], F32))
                r_b = CS(nc.sbuf_tensor(f"r_bC{h}", [128, TH], BF16))
                mr_b = CS(nc.sbuf_tensor(f"mr_bC{h}", [128, TH], BF16))

                sch.emit("s", lambda s: _load_act_set(
                    s, nc, "natural_log_exp_and_others"))
                g_o = {}; t_o = {}; v_ev = {}; s_sq = {}; t_st = {}
                stat_pend = []
                for d in range(NDC):
                    sl = d % 2
                    if h == 0 and d % 2 == 0:
                        ci = d // 2
                        sch.emit("g", (lambda ci=ci: lambda g: g.dma_start(
                            bass.AP(wd8_h.tensor, ci * 2 * FI * 128,
                                    [[1, 1], [1, 2 * FI * 128]]),
                            bass.AP(wd_h.tensor, ci * 2 * FI * 128,
                                    [[1, 1], [1, 2 * FI * 128]])))(),
                            inc="gd8", inc_n=16)
                    w = {} if d < 2 else {"s_t": t_o[d - 2]}
                    c = sch.emit("g", (lambda d=d, sl=sl: lambda g: g.dma_start(
                        ost[:, sl * INNER:(sl + 1) * INNER],
                        bass.AP(wout_h.tensor, d * INNER * 128,
                                [[128, 128], [128 * 128, NCC], [1, 128]])))(),
                        waits=w, inc=f"go{sl}", inc_n=16)
                    g_o[d] = c[f"go{sl}"]

                    def out_fn(d=d, sl=sl):
                        def fn(t):
                            last = None
                            for k in range(NCC):
                                for sub in range(2):
                                    last = t.matmul(
                                        pp[d % 2][:, sub * 512:(sub + 1) * 512],
                                        ost[:, sl * INNER + k * 128:
                                            sl * INNER + (k + 1) * 128],
                                        combined[:, k * TH + sub * 512:
                                                 k * TH + (sub + 1) * 512],
                                        start=(k == 0), stop=(k == NCC - 1))
                            return last
                        return fn
                    w = {f"go{sl}": g_o[d]}
                    if d < 2:
                        w.update(pgC)
                    if d >= 2:
                        w["s_v"] = v_ev[d - 2]
                        w["s_s"] = s_sq[d - 2]
                    c = sch.emit("t", out_fn(), waits=w, inc="s_t")
                    t_o[d] = c["s_t"]
                    c = sch.emit("v", (lambda d=d: lambda v: v.tensor_copy(
                        attnb[:, d * TH:(d + 1) * TH], pp[d % 2][:, :]))(),
                        waits={"s_t": t_o[d]}, inc="s_v")
                    v_ev[d] = c["s_v"]
                    if DBG_C_LEVEL < 2:
                        s_sq[d] = sch.cnt["s_s"]
                        t_st[d] = sch.cnt["s_t"]
                        continue
                    if DBG_C_LEVEL == 12:
                        w = {"s_t": t_o[d]}
                        c = sch.emit("s", (lambda d=d, sl=sl: lambda s:
                                      s.activation(
                                          sqrot[:, sl * TH:(sl + 1) * TH],
                                          pp[d % 2][:, :], AF.Square))(),
                                     waits=w, inc="s_s")
                        s_sq[d] = c["s_s"]
                        t_st[d] = sch.cnt["s_t"]
                        continue
                    if DBG_C_LEVEL == 13:
                        def statA_fn(d=d):
                            def fn(t):
                                last = None
                                for sub in range(2):
                                    last = t.matmul(
                                        pp[2][:, sub * 512:(sub + 1) * 512],
                                        ones_bf[:, :],
                                        attnb[:, d * TH + sub * 512:
                                              d * TH + (sub + 1) * 512],
                                        start=(d == 0), stop=(d == NDC - 1))
                                return last
                            return fn
                        c = sch.emit("t", statA_fn(),
                                     waits={"s_v": v_ev[d]}, inc="s_t")
                        t_st[d] = c["s_t"]
                        s_sq[d] = sch.cnt["s_s"]
                        continue
                    w = {"s_v": v_ev[d]}
                    if d >= 2:
                        w["s_t"] = t_st.get(d - 2) or t_st[d - 3]
                    c = sch.emit("s", (lambda d=d, sl=sl: lambda s: s.activation(
                        sqrot[:, sl * TH:(sl + 1) * TH],
                        attnb[:, d * TH:(d + 1) * TH],
                        AF.Square))(), waits=w, inc="s_s")
                    s_sq[d] = c["s_s"]

                    def stat_fn(d=d, sl=sl):
                        def fn(t):
                            last = None
                            for sub in range(2):
                                last = t.matmul(
                                    pp[2][:, sub * 512:(sub + 1) * 512],
                                    ones_bf[:, :],
                                    attnb[:, d * TH + sub * 512:
                                          d * TH + (sub + 1) * 512],
                                    start=(d == 0), stop=(d == NDC - 1))
                            for sub in range(2):
                                last = t.matmul(
                                    pp[3][:, sub * 512:(sub + 1) * 512],
                                    ones_bf[:, :],
                                    sqrot[:, sl * TH + sub * 512:
                                          sl * TH + (sub + 1) * 512],
                                    start=(d == 0), stop=(d == NDC - 1))
                            return last
                        return fn
                    stat_pend.append((d, stat_fn()))
                    if d >= 1:
                        pd, pfn = stat_pend.pop(0)
                        c = sch.emit("t", pfn,
                                     waits={"s_v": v_ev[pd], "s_s": s_sq[pd]},
                                     inc="s_t")
                        t_st[pd] = c["s_t"]

                if DBG_C_LEVEL >= 3:
                    pd, pfn = stat_pend.pop(0)
                c = sch.emit("t", pfn,
                             waits={"s_v": v_ev[pd], "s_s": s_sq[pd]},
                             inc="s_t")
                t_st[pd] = c["s_t"]
                stats_math((m_b, va_b, vb_b, r_b, mr_b), pp[2], pp[3], t_st[NDC - 1])

                ost_st.close()
                comb_st.close()
                x2b_st = ExitStack()
                x2b = x2b_st.enter_context(
                    nc.sbuf_tensor(f"x2b{h}", [128, NDC * TH], BF16,
                                   side="right"))

                v_x2 = {}; s_sq2 = {}; t_st2 = {}; y_sp = {}
                for d in range(NDC if DBG_C_LEVEL >= 4 else 0):
                    sl = d % 4
                    w = dict(pgC) if d < 4 else {"s_v": v_x2[d - 4]}
                    c = sch.emit("g", (lambda d=d, sl=sl: lambda g: g.dma_start(
                        xrot[:, sl * TH:(sl + 1) * TH],
                        x_h[d * 128:(d + 1) * 128, t0:t0 + TH]))(),
                        waits=w, inc=f"gx{sl}", inc_n=16)
                    gx = c[f"gx{sl}"]

                    def x2_fn(d=d, sl=sl):
                        def fn(v):
                            v.tensor_mul(sqrot[:, (d % 2) * TH:(d % 2 + 1) * TH],
                                         attnb[:, d * TH:(d + 1) * TH],
                                         r_b[:, :])
                            v.tensor_sub(sqrot[:, (d % 2) * TH:(d % 2 + 1) * TH],
                                         sqrot[:, (d % 2) * TH:(d % 2 + 1) * TH],
                                         mr_b[:, :])
                            return v.tensor_add(
                                x2b[:, d * TH:(d + 1) * TH],
                                sqrot[:, (d % 2) * TH:(d % 2 + 1) * TH],
                                xrot[:, sl * TH:(sl + 1) * TH])
                        return fn
                    w = {f"gx{sl}": gx}
                    c = sch.emit("v", x2_fn(), waits=w, n=3)
                    v_x2[d] = c["s_v"]
                    w = {"s_v": v_x2[d]}
                    if d >= 2:
                        w["s_t"] = t_st2[d - 2]
                    c = sch.emit("s", (lambda d=d: lambda s: s.activation(
                        sq2rot[:, (d % 2) * TH:(d % 2 + 1) * TH],
                        x2b[:, d * TH:(d + 1) * TH], AF.Square))(),
                        waits=w, inc="s_s")
                    s_sq2[d] = c["s_s"]

                    def stat2_fn(d=d, sl=sl):
                        def fn(t):
                            last = None
                            for sub in range(2):
                                last = t.matmul(
                                    pp[0][:, sub * 512:(sub + 1) * 512],
                                    ones_bf[:, :],
                                    x2b[:, d * TH + sub * 512:
                                        d * TH + (sub + 1) * 512],
                                    start=(d == 0), stop=(d == NDC - 1))
                            for sub in range(2):
                                last = t.matmul(
                                    pp[1][:, sub * 512:(sub + 1) * 512],
                                    ones_bf[:, :],
                                    sq2rot[:, (d % 2) * TH + sub * 512:
                                           (d % 2) * TH + (sub + 1) * 512],
                                    start=(d == 0), stop=(d == NDC - 1))
                            return last
                        return fn
                    w = {"s_v": v_x2[d], "s_s": s_sq2[d]}
                    if d < 2:
                        w["s_v"] = max(w["s_v"], v_ev[NDC - 1])
                    c = sch.emit("t", stat2_fn(), waits=w, inc="s_t")
                    t_st2[d] = c["s_t"]
                    c = sch.emit("y", (lambda d=d: lambda y: y.dma_start(
                        x2scr_h[d * 128:(d + 1) * 128, t0:t0 + TH],
                        x2b[:, d * TH:(d + 1) * TH]))(),
                        waits={"s_v": v_x2[d]},
                        inc=f"ys{d % 2}", inc_n=16)
                    y_sp[d] = c[f"ys{d % 2}"]

                if DBG_C_LEVEL >= 5:
                    v_st2 = stats_math((m_b, va_b, vb_b, r_b, mr_b),
                                       pp[0], pp[1], t_st2[NDC - 1])

                v_l2 = {}
                for d in range(NDC if DBG_C_LEVEL >= 5 else 0):
                    sl = d % 2

                    def ln2_fn(d=d, sl=sl):
                        def fn(v):
                            v.tensor_mul(sq2rot[:, sl * TH:(sl + 1) * TH],
                                         x2b[:, d * TH:(d + 1) * TH],
                                         r_b[:, :])
                            return v.tensor_sub(
                                ln2b[:, d * TH:(d + 1) * TH],
                                sq2rot[:, sl * TH:(sl + 1) * TH],
                                mr_b[:, :])
                        return fn
                    c = sch.emit("v", ln2_fn(), waits={"s_v": v_st2}, n=2)
                    v_l2[d] = c["s_v"]
                pg_xrot = sch.guard()
                sch.barrier(exclude=("g", "y", "t", "s"))
                sch.flush()
            x2b_st.close()
            if _done():
                cd_st.close(); break

            pgD = sch.guard()
            fuse_st = ExitStack()
            fuse = fuse_st.enter_context(
                nc.sbuf_tensor(f"fuse{h}", [128, NIC * TH], BF16, side="right"))
            with ExitStack() as d_st:
                DS = d_st.enter_context
                g8 = DS(nc.sbuf_tensor(f"g8{h}", [128, 4 * D], FP8, side="right"))
                u8 = DS(nc.sbuf_tensor(f"u8{h}", [128, 4 * D], FP8, side="right"))
                gtmp = DS(nc.sbuf_tensor(f"gtmp{h}", [128, 2 * TH], BF16))
                utmp = DS(nc.sbuf_tensor(f"utmp{h}", [128, 2 * TH], BF16))

                sch.emit("s", lambda s: _load_act_set(s, nc, "silu_and_others"))
                gb_c = {}; ub_c = {}; t_g = {}; t_u = {}
                s_si = {}; v_ue = {}; v_fu = {}
                q_g = {}; q_u = {}

                def d_load(i):
                    sl = i % 4
                    w = {"gq8": 128}
                    if i >= 4:
                        w["s_t"] = t_g[i - 4]
                    c = sch.emit("y", (lambda i=i, sl=sl: lambda y: y.dma_start(
                        g8[:, sl * D:(sl + 1) * D],
                        bass.AP(wg8_h.tensor, i * D * 128,
                                [[128, 128], [128 * 128, NDC], [1, 128]])))(),
                        waits=w, inc=f"gb{sl}", inc_n=16)
                    gb_c[i] = c[f"gb{sl}"]
                    c = sch.emit("v", (lambda sl=sl: lambda v: v.tensor_copy(
                        g8[:, sl * D:(sl + 1) * D].bitcast(mybir.dt.uint32),
                        g8[:, sl * D:(sl + 1) * D].bitcast(mybir.dt.uint32)))(),
                        waits={f"gb{sl}": gb_c[i]})
                    q_g[i] = c["s_v"]
                    w = {"gu8": 128}
                    if i >= 4:
                        w["s_t"] = t_u[i - 4]
                    c = sch.emit("y", (lambda i=i, sl=sl: lambda y: y.dma_start(
                        u8[:, sl * D:(sl + 1) * D],
                        bass.AP(wu8_h.tensor, i * D * 128,
                                [[128, 128], [128 * 128, NDC], [1, 128]])))(),
                        waits=w, inc=f"gu{sl}", inc_n=16)
                    ub_c[i] = c[f"gu{sl}"]
                    c = sch.emit("v", (lambda sl=sl: lambda v: v.tensor_copy(
                        u8[:, sl * D:(sl + 1) * D].bitcast(mybir.dt.uint32),
                        u8[:, sl * D:(sl + 1) * D].bitcast(mybir.dt.uint32)))(),
                        waits={f"gu{sl}": ub_c[i]})
                    q_u[i] = c["s_v"]

                d_load(0)
                d_load(1)
                for i in range(NIC):
                    sl = i % 4
                    if i + 2 < NIC:
                        d_load(i + 2)

                    w = {f"gb{sl}": gb_c[i], "s_v": q_g[i]}
                    if i < 2:
                        w.update({k: max(w.get(k, 0), v) for k, v in pgD.items()})
                    if i >= 2:
                        w["s_s"] = s_si[i - 2]
                    c = sch.emit("t", mm_group(i % 2, g8, sl, NDC, ln2b),
                                 waits=w, inc="s_t")
                    t_g[i] = c["s_t"]
                    w = {f"gu{sl}": ub_c[i], "s_v": q_u[i]}
                    if i < 2:
                        w.update({k: max(w.get(k, 0), v) for k, v in pgD.items()})
                    if i >= 2:
                        w["s_v"] = max(w["s_v"], v_ue[i - 2])
                    c = sch.emit("t", mm_group(2 + i % 2, u8, sl, NDC, ln2b),
                                 waits=w, inc="s_t")
                    t_u[i] = c["s_t"]
                    w = {"s_t": t_g[i]}
                    if i >= 2:
                        w["s_v"] = v_fu[i - 2]
                    c = sch.emit("s", (lambda i=i: lambda s: s.activation(
                        gtmp[:, (i % 2) * TH:(i % 2 + 1) * TH], pp[i % 2][:, :],
                        AF.Silu, scale=gsc[:, i:i + 1]))(),
                        waits=w, inc="s_s")
                    s_si[i] = c["s_s"]
                    w = {"s_t": t_u[i]}
                    if i >= 2:
                        w["s_v"] = v_fu[i - 2]
                    c = sch.emit("v", (lambda i=i: lambda v:
                                  v.tensor_scalar_mul(
                                      utmp[:, (i % 2) * TH:(i % 2 + 1) * TH],
                                      pp[2 + i % 2][:, :], usc[:, i:i + 1]))(),
                                 waits=w, inc="s_v")
                    v_ue[i] = c["s_v"]
                    c = sch.emit("v", (lambda i=i: lambda v: v.tensor_mul(
                        fuse[:, i * TH:(i + 1) * TH],
                        gtmp[:, (i % 2) * TH:(i % 2 + 1) * TH],
                        utmp[:, (i % 2) * TH:(i % 2 + 1) * TH]))(),
                        waits={"s_s": s_si[i]}, inc="s_v")
                    v_fu[i] = c["s_v"]
                sch.barrier(exclude=("g", "y", "t", "s"))
                sch.flush()
            if _done():
                cd_st.close(); fuse_st.close(); break

            pgE = sch.guard()
            cd_st.close()
            with ExitStack() as e_st:
                ES = e_st.enter_context
                d8 = ES(nc.sbuf_tensor(f"d8{h}", [128, 2 * FI], FP8, side="right"))
                ffb = ES(nc.sbuf_tensor(f"ffb{h}", [128, NDC * TH], BF16))
                sqrot = ES(nc.sbuf_tensor(f"sqrotE{h}", [128, 2 * TH], BF16))
                x2rot = ES(nc.sbuf_tensor(f"x2rotE{h}", [128, 4 * TH], BF16))
                outst = ES(nc.sbuf_tensor(f"outst{h}", [128, 2 * TH], F32))
                m_b = ES(nc.sbuf_tensor(f"m_bE{h}", [128, TH], F32))
                va_b = ES(nc.sbuf_tensor(f"va_bE{h}", [128, TH], F32))
                vb_b = ES(nc.sbuf_tensor(f"vb_bE{h}", [128, TH], F32))
                r_b = ES(nc.sbuf_tensor(f"r_bE{h}", [128, TH], BF16))
                mr_b = ES(nc.sbuf_tensor(f"mr_bE{h}", [128, TH], BF16))

                sch.emit("s", lambda s: _load_act_set(s, nc, "silu_and_others"))
                gd_c = {}; t_d = {}; s_ff = {}; s_sq = {}; t_st = {}
                q_d = {}
                stat_pendE = []
                for d in range(NDC):
                    sl = d % 2
                    w = {"gd8": 128}
                    if d >= 2:
                        w["s_t"] = t_d[d - 2]
                    c = sch.emit("y", (lambda d=d, sl=sl: lambda y: y.dma_start(
                        d8[:, sl * FI:(sl + 1) * FI],
                        bass.AP(wd8_h.tensor, d * FI * 128,
                                [[128, 128], [128 * 128, NIC], [1, 128]])))(),
                        waits=w, inc=f"gd{sl}", inc_n=16)
                    gd_c[d] = c[f"gd{sl}"]
                    c = sch.emit("v", (lambda sl=sl: lambda v: v.tensor_copy(
                        d8[:, sl * FI:(sl + 1) * FI].bitcast(mybir.dt.uint32),
                        d8[:, sl * FI:(sl + 1) * FI].bitcast(mybir.dt.uint32)))(),
                        waits={f"gd{sl}": gd_c[d]})
                    q_d[d] = c["s_v"]

                    def dn_fn(d=d, sl=sl):
                        def fn(t):
                            last = None
                            for k in range(NIC):
                                for sub in range(2):
                                    last = t.matmul(
                                        pp[d % 2][:, sub * 512:(sub + 1) * 512],
                                        d8[:, sl * FI + k * 128:
                                           sl * FI + (k + 1) * 128],
                                        fuse[:, k * TH + sub * 512:
                                             k * TH + (sub + 1) * 512],
                                        start=(k == 0), stop=(k == NIC - 1))
                            return last
                        return fn
                    w = {f"gd{sl}": gd_c[d], "s_v": q_d[d]}
                    if d < 2:
                        w.update({k: max(w.get(k, 0), v) for k, v in pgE.items()})
                    if d >= 2:
                        w["s_s"] = s_ff[d - 2]
                    c = sch.emit("t", dn_fn(), waits=w, inc="s_t")
                    t_d[d] = c["s_t"]
                    c = sch.emit("s", (lambda d=d: lambda s: s.activation(
                        ffb[:, d * TH:(d + 1) * TH], pp[d % 2][:, :],
                        AF.Identity, scale=dsc[:, d:d + 1]))(),
                        waits={"s_t": t_d[d]}, inc="s_s")
                    s_ff[d] = c["s_s"]
                    w = {"s_s": s_ff[d]}
                    if d >= 2:
                        w["s_t"] = t_st.get(d - 2) or t_st[d - 3]
                    c = sch.emit("s", (lambda d=d, sl=sl: lambda s: s.activation(
                        sqrot[:, sl * TH:(sl + 1) * TH],
                        ffb[:, d * TH:(d + 1) * TH], AF.Square))(),
                        waits=w, inc="s_s")
                    s_sq[d] = c["s_s"]

                    def stat_fn(d=d, sl=sl):
                        def fn(t):
                            last = None
                            for sub in range(2):
                                last = t.matmul(
                                    pp[2][:, sub * 512:(sub + 1) * 512],
                                    ones_bf[:, :],
                                    ffb[:, d * TH + sub * 512:
                                        d * TH + (sub + 1) * 512],
                                    start=(d == 0), stop=(d == NDC - 1))
                            for sub in range(2):
                                last = t.matmul(
                                    pp[3][:, sub * 512:(sub + 1) * 512],
                                    ones_bf[:, :],
                                    sqrot[:, sl * TH + sub * 512:
                                          sl * TH + (sub + 1) * 512],
                                    start=(d == 0), stop=(d == NDC - 1))
                            return last
                        return fn
                    stat_pendE.append((d, stat_fn()))
                    if d >= 1:
                        pd, pfn = stat_pendE.pop(0)
                        c = sch.emit("t", pfn,
                                     waits={"s_s": s_sq[pd]}, inc="s_t")
                        t_st[pd] = c["s_t"]

                pd, pfn = stat_pendE.pop(0)
                c = sch.emit("t", pfn, waits={"s_s": s_sq[pd]}, inc="s_t")
                t_st[pd] = c["s_t"]
                sch.emit("s", lambda s: _load_act_set(
                    s, nc, "natural_log_exp_and_others"),
                    waits={"s_t": t_st[NDC - 1]})
                v_st3 = stats_math((m_b, va_b, vb_b, r_b, mr_b), pp[2], pp[3], t_st[NDC - 1])
                v_st3_prev = v_st3

                v_o = {}; y_o = {}
                for d in range(NDC):
                    sl = d % 4
                    w = dict(pgE) if d < 4 else {"s_v": v_o[d - 4]}
                    c = sch.emit("y" if d % 2 == 0 else "g",
                                 (lambda d=d, sl=sl: lambda q: q.dma_start(
                                     x2rot[:, sl * TH:(sl + 1) * TH],
                                     x2scr_h[d * 128:(d + 1) * 128,
                                             t0:t0 + TH]))(),
                                 waits=w, inc=f"yr{sl}", inc_n=16)
                    yx = c[f"yr{sl}"]

                    def fin_fn(d=d, sl=sl):
                        def fn(v):
                            v.tensor_mul(sqrot[:, (d % 2) * TH:(d % 2 + 1) * TH],
                                         ffb[:, d * TH:(d + 1) * TH],
                                         r_b[:, :])
                            v.tensor_sub(sqrot[:, (d % 2) * TH:(d % 2 + 1) * TH],
                                         sqrot[:, (d % 2) * TH:(d % 2 + 1) * TH],
                                         mr_b[:, :])
                            return v.tensor_add(
                                outst[:, (d % 2) * TH:(d % 2 + 1) * TH],
                                sqrot[:, (d % 2) * TH:(d % 2 + 1) * TH],
                                x2rot[:, sl * TH:(sl + 1) * TH])
                        return fn
                    w = {f"yr{sl}": yx, "s_v": v_st3}
                    if d >= 2:
                        w[f"yo{d % 2}"] = y_o[d - 2]
                        w["s_v"] = max(w["s_v"], v_o[d - 2])
                    c = sch.emit("v", fin_fn(), waits=w, n=3)
                    v_o[d] = c["s_v"]
                    c = sch.emit("y" if d % 2 == 0 else "g",
                                 (lambda d=d: lambda q: q.dma_start(
                                     out_h[d * 128:(d + 1) * 128, t0:t0 + TH],
                                     outst[:, (d % 2) * TH:(d % 2 + 1) * TH]))(),
                                 waits={"s_v": v_o[d]}, inc=f"yo{d % 2}",
                                 inc_n=16)
                    y_o[d] = c[f"yo{d % 2}"]
                sch.barrier(exclude=("g", "y", "t", "s"))
                sch.flush()
            fuse_st.close()
            if _done():
                break

        sch.emit("y", None, waits={"yo0": sch.cnt["yo0"], "yo1": sch.cnt["yo1"]})
        sch.flush()
    return nc


def _prep_inputs(kw):
    x = np.asarray(kw["L0_attn_hidden_states"], dtype=np.float32)
    in_w = np.asarray(kw["L0_attn_in_proj_w"], dtype=np.float32)
    gt_w = np.asarray(kw["L0_attn_gate_proj_w"], dtype=np.float32)
    out_w = np.asarray(kw["L0_attn_out_proj_w"], dtype=np.float32)
    conv_w = np.asarray(kw["L0_attn_conv_w"], dtype=np.float32)
    conv_b = np.asarray(kw["L0_attn_conv_b"], dtype=np.float32)
    dt_w = np.asarray(kw["L0_attn_dt_w"], dtype=np.float32)
    dt_b = np.asarray(kw["L0_attn_dt_b"], dtype=np.float32)
    A_log = np.asarray(kw["L0_attn_A_log"], dtype=np.float32)
    Bp = np.asarray(kw["L0_attn_Bp"], dtype=np.float32)
    Cp = np.asarray(kw["L0_attn_Cp"], dtype=np.float32)
    Dp = np.asarray(kw["L0_attn_Dp"], dtype=np.float32)
    gate_w = np.asarray(kw["L0_ffn_gate_w"], dtype=np.float32)
    up_w = np.asarray(kw["L0_ffn_up_w"], dtype=np.float32)
    down_w = np.asarray(kw["L0_ffn_down_w"], dtype=np.float32)
    gate_sc = np.asarray(kw["L0_ffn_gate_scale"], dtype=np.float32)
    up_sc = np.asarray(kw["L0_ffn_up_scale"], dtype=np.float32)
    down_sc = np.asarray(kw["L0_ffn_down_scale"], dtype=np.float32)

    win = np.ascontiguousarray(in_w.reshape(NCC, 128, D).transpose(0, 2, 1))
    wgt = np.ascontiguousarray(gt_w.reshape(NCC, 128, D).transpose(0, 2, 1))
    wout = np.ascontiguousarray(out_w.reshape(NDC, 128, INNER).transpose(0, 2, 1))
    wg = np.ascontiguousarray(gate_w.reshape(D, NIC, 128).transpose(1, 0, 2))
    wu = np.ascontiguousarray(up_w.reshape(D, NIC, 128).transpose(1, 0, 2))
    wd = np.ascontiguousarray(down_w.reshape(FI, NDC, 128).transpose(1, 0, 2))

    dcv = np.zeros((NCC, 4, 128, 128), dtype=np.float32)
    idx = np.arange(128)
    dcv[:, :, idx, idx] = conv_w.reshape(NCC, 128, 4).transpose(0, 2, 1)

    dtw = np.zeros((NCC, 128, 128), dtype=np.float32)
    for cc in range(NCC):
        for hh in range(2):
            hd = 2 * cc + hh
            blk = np.broadcast_to(dt_w[hd][:, None], (64, 64))
            dtw[cc, hh * 64:(hh + 1) * 64, hh * 64:(hh + 1) * 64] = blk

    chan = np.zeros((128, 320), dtype=np.float32)

    def per_chan(vec):
        return vec.reshape(NCC, 128).T
    chan[:, 0:32] = per_chan(A_log.reshape(-1))
    chan[:, 32:64] = per_chan(Bp.reshape(-1))
    chan[:, 64:96] = per_chan(Cp.reshape(-1))
    chan[:, 96:128] = per_chan(Dp.reshape(-1))
    chan[:, 128:160] = per_chan(np.repeat(dt_b, 64))
    chan[:, 160:192] = per_chan(conv_b)
    for k in range(4):
        chan[:, 192 + k * 32:192 + (k + 1) * 32] = per_chan(conv_w[:, k])

    gsc = np.ascontiguousarray(gate_sc.reshape(NIC, 128).T)
    usc = np.ascontiguousarray(up_sc.reshape(NIC, 128).T)
    dsc = np.ascontiguousarray(down_sc.reshape(NDC, 128).T)

    shared = dict(win=win, wgt=wgt, wout=wout, wg=wg, wu=wu, wd=wd,
                  dtw=dtw, dcv=dcv, chanv=chan, gsc=gsc, usc=usc, dsc=dsc)
    xT = np.ascontiguousarray(x.transpose(0, 2, 1))
    in_maps = [dict(shared, x=xT[b]) for b in range(x.shape[0])]
    return in_maps, x.shape


def kernel(**kw):
    in_maps, xshape = _prep_inputs(kw)
    nc = build_nc()
    res = run_bass_kernel_spmd(nc, in_maps, core_ids=list(range(len(in_maps))))
    B = xshape[0]
    out = np.empty(xshape, dtype=np.float32)
    for b in range(B):
        out[b] = res.results[b]["out"].T
    return out
